# version 08
# baseline (speedup 1.0000x reference)
import numpy as np
import ml_dtypes
from contextlib import ExitStack

import concourse.bass as bass
import concourse.mybir as mybir
from concourse.bass_utils import run_bass_kernel_spmd

F32 = mybir.dt.float32
BF16 = mybir.dt.bfloat16
U32 = mybir.dt.uint32
ALU = mybir.AluOpType
AF = mybir.ActivationFunctionType
AX = mybir.AxisListType

CORES = 8
TOK = 1024
D = 512
H = 4096
DFF = 2048
FI = 11008
NL = 4
NT_G = 86
W_CHUNKS = [11, 11, 11, 11, 11, 11, 11, 9]
CH_START = [sum(W_CHUNKS[:i]) for i in range(8)]
EPS = 1e-5
CSZ = 32
JCH = TOK // CSZ
G_SCHED = [list(range(CH_START[l], CH_START[l] + W_CHUNKS[l])) for l in range(4)]
G_POST = list(range(CH_START[4], 86))

DEBUG = False


class Sem:
    def __init__(self, h):
        self.h = h
        self.n = 0


class Marks:
    def __init__(self, prev=None):
        self.d = {}
        self.prev = prev

    def mark(self, name, sem):
        self.d[name] = sem.n

    def get(self, name):
        return 0 if self.prev is None else self.prev.d[name]


def build(marks_prev=None):
    M = Marks(marks_prev)
    nc = bass.Bass("TRN2", target_bir_lowering=False)

    def din(name, shape, dt=F32):
        return nc.dram_tensor(name, shape, dt, kind="ExternalInput").ap()

    x_in = din("x_tok", [TOK, H])
    winT = din("winT", [H, D])
    w1T = din("w1T", [NL, D, DFF])
    w2T = din("w2T", [NL, DFF, D])
    woutT = din("woutT", [D, H])
    bin_p = din("bin_p", [128, 4])
    b1_p = din("b1_p", [128, NL * 16])
    b2_p = din("b2_p", [128, NL * 4])
    bout_p = din("bout_p", [128, 32])
    ln1w_p = din("ln1w_p", [128, NL * 4])
    ln1b_p = din("ln1b_p", [128, NL * 4])
    ln2w_p = din("ln2w_p", [128, NL * 4])
    ln2b_p = din("ln2b_p", [128, NL * 4])
    scl_p = din("scl_p", [128, NL * 4])
    pr_p = din("pr_p", [128, NL * 4])
    pi_p = din("pi_p", [128, NL * 4])
    lcr_p = din("lcr_p", [128, NL * 4])
    lci_p = din("lci_p", [128, NL * 4])
    lnlw_p = din("lnlw_p", [128, 4])
    lnlb_p = din("lnlb_p", [128, 4])
    qwgu = din("qwgu", [64, 2 * FI], U32)
    scgu = din("scgu", [4, 2 * FI])
    qzgu = din("qzgu", [4, 2 * FI // 8], U32)
    qwd = din("qwd", [1376, 512], U32)
    scd = din("scd", [86, 512])
    qzd = din("qzd", [86, 64], U32)
    idb = din("idb", [128, 128], BF16)
    idf = din("idf", [128, 128], F32)
    ones512 = din("ones512", [128, 128])
    sel = din("sel", [1, 8])
    ceps = din("ceps", [128, 1])
    cmh = din("cmh", [128, 1])

    out_d = nc.dram_tensor("out", [TOK, H], F32, kind="ExternalOutput").ap()

    dbg = "ExternalOutput" if DEBUG else "Internal"
    aggu_in = [nc.dram_tensor(f"aggu_in{i}", [512, 2 * W_CHUNKS[i] * 128], BF16)
               for i in range(8)]
    aggu_out = [nc.dram_tensor(f"aggu_out{i}", [CORES * 512, 2 * W_CHUNKS[i] * 128],
                               BF16, kind=dbg,
                               addr_space=("Local" if DEBUG else "Shared")) for i in range(8)]
    agd_in = nc.dram_tensor("agd_in", [FI, 512], BF16)
    agd_out = nc.dram_tensor("agd_out", [CORES * FI, 512], BF16, kind=dbg,
                             addr_space=("Local" if DEBUG else "Shared"))
    h_dram = nc.dram_tensor("h_dram", [NT_G, 128, TOK], BF16, kind=dbg)
    at_dram = nc.dram_tensor("at_dram", [32, 128, TOK], BF16, kind=dbg)
    dqsc_dram = nc.dram_tensor("dqsc_dram", [4, 1408], BF16)
    dqszq_dram = nc.dram_tensor("dqszq_dram", [4, 1408], BF16)
    ddsc_dram = nc.dram_tensor("ddsc_dram", [86, 512], BF16)
    ddszq_dram = nc.dram_tensor("ddszq_dram", [86, 512], BF16)
    agf_in = [nc.dram_tensor(f"agf_in{l}", [1, 1024], F32) for l in range(NL)]
    agf_out = [nc.dram_tensor(f"agf_out{l}", [CORES, 1024], F32) for l in range(NL)]

    with ExitStack() as ctx:
        def sbuf(name, shape, dt):
            return ctx.enter_context(nc.sbuf_tensor(name, shape, dt))

        def psum(name, shape, dt):
            return ctx.enter_context(nc.psum_tensor(name, shape, dt))

        def sem(name):
            return Sem(ctx.enter_context(nc.semaphore(name)))

        arenaA = sbuf("arenaA", [128, 32 * TOK], BF16)
        arenaG = sbuf("arenaG", [128, 24576], BF16)
        bufC = sbuf("bufC", [128, H], F32)
        bufD = sbuf("bufD", [128, H], F32)
        bufE = sbuf("bufE", [128, H], F32)
        bufF = sbuf("bufF", [128, 2048], F32)
        slabs = sbuf("slabs", [128, 12288], BF16)
        htst = sbuf("htst", [128, 2048], BF16)
        p_bin = sbuf("p_bin", [128, 4], F32)
        p_b1 = sbuf("p_b1", [128, NL * 16], F32)
        p_b2 = sbuf("p_b2", [128, NL * 4], F32)
        p_bout = sbuf("p_bout", [128, 32], F32)
        p_l1w = sbuf("p_l1w", [128, NL * 4], F32)
        p_l1b = sbuf("p_l1b", [128, NL * 4], F32)
        p_l2w = sbuf("p_l2w", [128, NL * 4], F32)
        p_l2b = sbuf("p_l2b", [128, NL * 4], F32)
        p_scl = sbuf("p_scl", [128, NL * 4], F32)
        p_pr = sbuf("p_pr", [128, NL * 4], F32)
        p_pi = sbuf("p_pi", [128, NL * 4], F32)
        p_lcr = sbuf("p_lcr", [128, NL * 4], F32)
        p_lci = sbuf("p_lci", [128, NL * 4], F32)
        p_lnlw = sbuf("p_lnlw", [128, 4], F32)
        p_lnlb = sbuf("p_lnlb", [128, 4], F32)
        idb_sb = sbuf("idb_sb", [128, 128], BF16)
        idf_sb = sbuf("idf_sb", [128, 128], F32)
        ones_sb = sbuf("ones_sb", [128, 128], F32)
        selb_sb = sbuf("selb_sb", [128, 8], F32)
        ab_sb = sbuf("ab_sb", [128, 16], F32)
        sqa_sb = sbuf("sqa_sb", [128, 44], F32)
        sqb_sb = sbuf("sqb_sb", [128, 44], F32)
        P32a = sbuf("P32a", [128, 128], F32)
        P32b = sbuf("P32b", [128, 128], F32)
        Qpa = sbuf("Qpa", [128, 128], F32)
        Qpb = sbuf("Qpb", [128, 128], F32)
        Sua = sbuf("Sua", [128, 128], F32)
        Sva = sbuf("Sva", [128, 128], F32)
        Sub = sbuf("Sub", [128, 128], F32)
        Svb = sbuf("Svb", [128, 128], F32)
        tmpa = sbuf("tmpa", [128, 128], F32)
        tmpb = sbuf("tmpb", [128, 128], F32)
        tmpc = sbuf("tmpc", [128, 128], F32)
        tmpd = sbuf("tmpd", [128, 128], F32)
        fco = sbuf("fco", [128, 16], F32)
        sall = sbuf("sall", [128, 64], F32)
        stat1 = sbuf("stat1", [128, 1], F32)
        stat2 = sbuf("stat2", [128, 1], F32)
        stat3 = sbuf("stat3", [128, 1], F32)
        ceps_sb = sbuf("ceps_sb", [128, 1], F32)
        cmh_sb = sbuf("cmh_sb", [128, 1], F32)

        ps = [psum(f"ps{k}", [128, 512], F32) for k in range(8)]

        def psb(k):
            return ps[k].ap().bitcast(BF16)[:, 0:128]

        s_pe = sem("s_pe")
        s_dve = sem("s_dve")
        s_act = sem("s_act")
        s_aq = sem("s_aq")
        s_sq = sem("s_sq")
        s_cc = sem("s_cc")
        s_pl = sem("s_pl")
        s_vs = sem("s_vs")
        s_ss = sem("s_ss")
        s_x = [sem("s_x0"), sem("s_x1")]
        s_ox = sem("s_ox")
        s_os = [sem("s_os0"), sem("s_os1")]
        s_gs = [sem(f"s_gs{k}") for k in range(3)]
        s_ds = [sem(f"s_ds{k}") for k in range(4)]
        s_qo = sem("s_qo")
        s_do = sem("s_do")
        s_dl = sem("s_dl")
        s_hw = [sem("s_hw0"), sem("s_hw1")]
        s_aw = [sem("s_aw0"), sem("s_aw1")]
        s_ar = [sem("s_ar0"), sem("s_ar1")]
        s_fw = sem("s_fw")
        s_fg = sem("s_fg")

        def carve(base, byte_off, dt, shape):
            esz = {F32: 4, U32: 4, BF16: 2}[dt]
            assert byte_off % 4 == 0
            b = base.ap().bitcast(dt)
            return b[: shape[0], byte_off // esz: byte_off // esz + shape[1]]

        dq_qwc = lambda: carve(arenaA, 0, U32, [64, 1408])
        dq_pp = lambda: carve(arenaA, 5632, U32, [128, 1408])
        dq_ppb = [lambda m=m: carve(arenaA, 11264 + m * 2816, BF16, [128, 1408]) for m in range(4)]
        dq_scb = lambda: carve(arenaA, 22528, BF16, [128, 1408])
        dq_szqb = lambda: carve(arenaA, 25344, BF16, [128, 1408])
        dq_scc = lambda: carve(arenaA, 28160, F32, [4, 1408])
        dq_qzc = lambda: carve(arenaA, 33792, U32, [4, 176])
        dq_zqc = lambda: carve(arenaA, 34496, U32, [4, 1408])
        dq_zqf = lambda: carve(arenaA, 40128, F32, [4, 1408])
        dq_szqs = lambda: carve(arenaA, 45760, BF16, [4, 1408])
        dq_scs = lambda: carve(arenaA, 48576, BF16, [4, 1408])
        dd_qwc = lambda: carve(arenaG, 0, U32, [128, 512])
        dd_pd = lambda: carve(arenaG, 2048, U32, [128, 512])
        dd_pdb = [lambda k=k: carve(arenaG, 4096 + k * 1024, BF16, [128, 512]) for k in range(8)]
        dd_scb = lambda: carve(arenaG, 12288, BF16, [128, 512])
        dd_szqb = lambda: carve(arenaG, 13312, BF16, [128, 512])
        dd_scd = lambda: carve(arenaG, 14336, F32, [86, 512])
        dd_qzd = lambda: carve(arenaG, 16384, U32, [86, 64])
        dd_zqd = lambda: carve(arenaG, 16640, U32, [86, 512])
        dd_zqdf = lambda: carve(arenaG, 18688, F32, [86, 512])
        dd_szqd = lambda: carve(arenaG, 20736, BF16, [86, 512])
        dd_scdb = lambda: carve(arenaG, 21760, BF16, [86, 512])
        winb = lambda: carve(arenaG, 0, BF16, [128, 16384])
        w1b = lambda: carve(arenaG, 0, BF16, [128, 8192])
        w2b = lambda: carve(arenaG, 16384, BF16, [128, 8192])
        y2b = lambda: carve(arenaG, 32768, BF16, [128, 4096])
        woutb = lambda: carve(arenaG, 0, BF16, [128, 16384])
        stg = [lambda k=k: carve(slabs, 0, F32, [128, 2048]) for k in range(2)]
        h1b = lambda: bufD.ap().bitcast(BF16)[:, 0:8192]
        gslab = [lambda k=k: carve(slabs, 8192 + k * 8192, BF16, [128, 4096]) for k in range(2)]
        wdsl = [lambda k=k: carve(slabs, k * 1024, BF16, [128, 512]) for k in range(4)]
        hsl = [lambda k=k: carve(slabs, 4096 + k * 2048, BF16, [128, 1024]) for k in range(4)]
        atsl = [lambda k=k: carve(htst, k * 2048, BF16, [128, 1024]) for k in range(2)]
        htsl = [lambda k=k: carve(htst, k * 2048, BF16, [128, 1024]) for k in range(2)]
        xnb = lambda: carve(bufE, 0, BF16, [128, H])
        gs = lambda pi_, hh: bufF.ap()[:, (pi_ % 2) * 1024 + hh * 512:(pi_ % 2) * 1024 + hh * 512 + 512]
        Sst2_u = sbuf("Sst2_u", [128, 264], F32)
        Sst2_v = sbuf("Sst2_v", [128, 264], F32)

        xnT = arenaA
        brsum = arenaA
        agf_sb = carve(arenaG, 40960, F32, [8, 1024])

        def pc(t, l, dp):
            return t.ap()[:, l * 4 + dp: l * 4 + dp + 1]

        cc_order = [("gu", 0), ("gu", 1), ("gu", 2), ("gu", 3), ("scan", 0),
                    ("gu", 4), ("scan", 1), ("gu", 5), ("scan", 2),
                    ("gu", 6), ("scan", 3), ("gu", 7), ("down",)]
        cc_idx = {e: k + 1 for k, e in enumerate(cc_order)}

        jobs = []
        for i in range(8):
            w = W_CHUNKS[i]
            jobs.append((i, 0, CH_START[i] * 128, w * 128))
            jobs.append((i, 1, FI + CH_START[i] * 128, w * 128))

        slab_seq = []
        for i in range(8):
            for t in range(W_CHUNKS[i]):
                slab_seq.append((i, t, 0))
                slab_seq.append((i, t, 1))

        with nc.Block() as block:
            @block.sync
            def _(e):
                def emit_slabs(ks):
                    for k in ks:
                        i, t, side = slab_seq[k]
                        if k >= 2:
                            e.wait_ge(s_pe.h, M.get(f"gu_mm_{k-2}"))
                        e.wait_ge(s_cc.h, cc_idx[("gu", i)])
                        w = W_CHUNKS[i]
                        coff = (t + side * w) * 128
                        src = aggu_out[i].ap().rearrange("(a p) c -> p a c", p=128)[:, :, coff:coff + 128]
                        sg = s_gs[k % 2]
                        e.dma_start(out=gslab[k % 2]().rearrange("p (a c) -> p a c", c=128),
                                    in_=src).then_inc(sg.h, 16)
                        sg.n += 16
                        M.mark(f"gu_slab_{k}", sg)

                smalls = [
                    (bin_p, p_bin), (b1_p, p_b1), (b2_p, p_b2), (bout_p, p_bout),
                    (ln1w_p, p_l1w), (ln1b_p, p_l1b), (ln2w_p, p_l2w), (ln2b_p, p_l2b),
                    (scl_p, p_scl), (pr_p, p_pr), (pi_p, p_pi), (lcr_p, p_lcr),
                    (lci_p, p_lci), (lnlw_p, p_lnlw), (lnlb_p, p_lnlb),
                    (idb, idb_sb), (idf, idf_sb), (ones512, ones_sb),
                    (ceps, ceps_sb), (cmh, cmh_sb),
                ]
                for src, dst in smalls:
                    e.dma_start(out=dst[:], in_=src).then_inc(s_sq.h, 16)
                    s_sq.n += 16
                e.dma_start(out=selb_sb[:], in_=sel.to_broadcast((128, 8))).then_inc(s_sq.h, 16)
                s_sq.n += 16
                M.mark("smalls", s_sq)

                for tt in range(8):
                    if tt >= 2:
                        e.wait_ge(s_dve.h, M.get(f"xn_done_{tt-2}"))
                    dst = bufC if tt % 2 == 0 else bufD
                    sx = s_x[tt % 2]
                    e.dma_start(out=dst[:], in_=x_in[tt * 128:(tt + 1) * 128, :]).then_inc(sx.h, 16)
                    sx.n += 16
                    M.mark(f"xload_{tt}", sx)

                for wc in range(8):
                    if wc >= 1:
                        e.wait_ge(s_dve.h, M.get(f"wincast_{wc-1}"))
                    e.dma_start(out=stg[wc % 2]().rearrange("p (a d) -> p a d", d=512),
                                in_=winT.rearrange("(a p) d -> p a d", p=128)
                                [:, wc * 4:(wc + 1) * 4, :]).then_inc(s_sq.h, 16)
                    s_sq.n += 16
                    M.mark(f"winload_{wc}", s_sq)

                for l in range(NL):
                    emit_slabs(range(2 * CH_START[l], 2 * CH_START[l] + 12))
                    for wc in range(4):
                        e.wait_ge(s_dve.h, M.get(f"w1cast_{l}_{wc}_free"))
                        e.dma_start(out=stg[wc % 2]().rearrange("p (a f) -> p a f", f=DFF),
                                    in_=w1T[l].rearrange("(a p) f -> p a f", p=128)
                                    [:, wc:wc + 1, :]).then_inc(s_sq.h, 16)
                        s_sq.n += 16
                        M.mark(f"w1load_{l}_{wc}", s_sq)
                    for wc in range(4):
                        e.wait_ge(s_dve.h, M.get(f"w2cast_{l}_{wc}_free"))
                        e.dma_start(out=stg[wc % 2]().rearrange("p (a d) -> p a d", d=512),
                                    in_=w2T[l].rearrange("(a p) d -> p a d", p=128)
                                    [:, wc * 4:(wc + 1) * 4, :]).then_inc(s_sq.h, 16)
                        s_sq.n += 16
                        M.mark(f"w2load_{l}_{wc}", s_sq)
                    emit_slabs(range(2 * CH_START[l] + 12, 2 * (CH_START[l] + W_CHUNKS[l])))

                for wc in range(8):
                    e.wait_ge(s_dve.h, M.get(f"woutcast_{wc}_free"))
                    e.dma_start(out=stg[wc % 2](),
                                in_=woutT.rearrange("(a p) v -> p a v", p=128)
                                [:, wc // 2, (wc % 2) * 2048:(wc % 2) * 2048 + 2048]).then_inc(s_sq.h, 16)
                    s_sq.n += 16
                    M.mark(f"woutload_{wc}", s_sq)

                emit_slabs(range(2 * CH_START[4], 172))

                for hg in range(8):
                    for ft in range(NT_G):
                        k = hg * NT_G + ft
                        if k == 0:
                            e.wait_ge(s_cc.h, cc_idx[("down",)])
                            e.wait_ge(s_hw[0].h, M.get("h_all_w0"))
                            e.wait_ge(s_hw[1].h, M.get("h_all_w1"))
                            e.wait_ge(s_dve.h, M.get("gating_done"))
                        if k >= 4:
                            e.wait_ge(s_pe.h, M.get(f"dn_mm_{k-4}"))
                        src_w = agd_out.ap()[hg * FI + ft * 128: hg * FI + ft * 128 + 128, :]
                        sd = s_ds[k % 4]
                        e.dma_start(out=wdsl[k % 4](), in_=src_w).then_inc(sd.h, 16)
                        e.dma_start(out=hsl[k % 4](), in_=h_dram.ap()[ft]).then_inc(sd.h, 16)
                        sd.n += 32
                        M.mark(f"dn_slab_{k}", sd)

                for tt in range(8):
                    if tt == 0:
                        e.wait_ge(s_dve.h, M.get("dn_evac_done"))
                    if tt >= 1:
                        e.wait_ge(s_dve.h, M.get(f"oadd_{tt-1}"))
                    e.dma_start(out=bufC[:], in_=x_in[tt * 128:(tt + 1) * 128, :]).then_inc(s_ox.h, 16)
                    s_ox.n += 16
                    M.mark(f"oxload_{tt}", s_ox)
                    e.wait_ge(s_dve.h, M.get(f"oadd_{tt}"))
                    dstbuf = bufD if tt % 2 == 0 else bufE
                    so = s_os[tt % 2]
                    e.dma_start(out=out_d[tt * 128:(tt + 1) * 128, :],
                                in_=dstbuf[:]).then_inc(so.h, 16)
                    so.n += 16
                    M.mark(f"ostore_{tt}", so)
                e.wait_ge(s_os[0].h, M.get("ostore_6"))
                e.wait_ge(s_os[1].h, M.get("ostore_7"))

            @block.scalar
            def _(e):
                ssn = [0]

                def ss(ins):
                    ins.then_inc(s_ss.h, 1)
                    ssn[0] += 1
                    e.wait_ge(s_ss.h, ssn[0])

                def _gsilu1(pi_):
                    for hh in range(2):
                        e.wait_ge(s_pe.h, M.get(f"gmm_{pi_}_{hh}"))
                        if pi_ >= 2 and hh == 0:
                            e.wait_ge(s_dve.h, M.get(f"gating_{pi_-2}"))
                        e.activation(out=gs(pi_, hh), in_=ps[4 + hh][:],
                                     func=AF.Silu).then_inc(s_act.h, 1)
                        s_act.n += 1
                        M.mark(f"gsilu_{pi_}_{hh}", s_act)

                def _hwr1(pi_):
                    e.wait_ge(s_dve.h, M.get(f"gating_{pi_}"))
                    if pi_ >= 2:
                        e.wait_ge(s_hw[pi_ % 2].h, M.get(f"hwr_{pi_-2}"))
                    sh = s_hw[pi_ % 2]
                    e.dma_start(out=h_dram.ap()[pi_], in_=htsl[pi_ % 2]()).then_inc(sh.h, 16)
                    sh.n += 16
                    M.mark(f"hwr_{pi_}", sh)

                def emit_g_scalar(pis):
                    pis = list(pis)
                    for j, pi_ in enumerate(pis):
                        _gsilu1(pi_)
                        if j >= 2:
                            _hwr1(pis[j - 2])
                    for pi_ in pis[-2:] if len(pis) >= 2 else pis:
                        _hwr1(pi_)

                e.wait_ge(s_sq.h, M.get("smalls"))
                for jn, (i, side, qcol, wcols) in enumerate(jobs):
                    e.dma_start(out=dq_qwc()[:, :wcols], in_=qwgu[:, qcol:qcol + wcols]).then_inc(s_aq.h, 16)
                    e.dma_start(out=dq_scc()[:, :wcols], in_=scgu[:, qcol:qcol + wcols]).then_inc(s_aq.h, 16)
                    e.dma_start(out=dq_qzc()[:, :wcols // 8],
                                in_=qzgu[:, qcol // 8:(qcol + wcols) // 8]).then_inc(s_aq.h, 16)
                    s_aq.n += 48
                    M.mark(f"dq_ld_{jn}", s_aq)
                    e.wait_ge(s_dve.h, M.get(f"dq_src_{jn}"))
                    e.dma_start(out=dqsc_dram.ap()[:, :wcols], in_=dq_scs()[:, :wcols]).then_inc(s_aq.h, 16)
                    e.dma_start(out=dqszq_dram.ap()[:, :wcols], in_=dq_szqs()[:, :wcols]).then_inc(s_aq.h, 16)
                    s_aq.n += 32
                    e.wait_ge(s_aq.h, s_aq.n)
                    for g in range(4):
                        for hh in range(2):
                            p0 = hh * 64 + g * 16
                            e.dma_start(out=dq_scb()[p0:p0 + 16, :wcols],
                                        in_=dqsc_dram.ap()[g:g + 1, :wcols].to_broadcast((16, wcols))).then_inc(s_aq.h, 16)
                            e.dma_start(out=dq_szqb()[p0:p0 + 16, :wcols],
                                        in_=dqszq_dram.ap()[g:g + 1, :wcols].to_broadcast((16, wcols))).then_inc(s_aq.h, 16)
                            s_aq.n += 32
                    M.mark(f"dq_bc_{jn}", s_aq)
                    e.wait_ge(s_dve.h, M.get(f"dq_scaled_{jn}"))
                    w = W_CHUNKS[i]
                    for m in range(4):
                        for hh in range(2):
                            kk = 2 * m + hh
                            dst = aggu_in[i].ap()[kk::8, side * w * 128: side * w * 128 + wcols]
                            e.dma_start(out=dst, in_=dq_ppb[m]()[hh * 64:hh * 64 + 64, :wcols]).then_inc(s_qo.h, 16)
                            s_qo.n += 16
                    M.mark(f"dq_out_{jn}", s_qo)
                e.dma_start(out=dd_scd(), in_=scd).then_inc(s_aq.h, 16)
                e.dma_start(out=dd_qzd(), in_=qzd).then_inc(s_aq.h, 16)
                s_aq.n += 32
                M.mark("dd_ld", s_aq)
                e.wait_ge(s_dve.h, M.get("dd_src"))
                e.dma_start(out=ddsc_dram.ap(), in_=dd_scdb()).then_inc(s_aq.h, 16)
                e.dma_start(out=ddszq_dram.ap(), in_=dd_szqd()).then_inc(s_aq.h, 16)
                s_aq.n += 32
                e.wait_ge(s_aq.h, s_aq.n)
                for pt in range(11):
                    rp = min(128, 1376 - pt * 128)
                    e.wait_ge(s_dve.h, M.get(f"dd_free_{pt}"))
                    e.dma_start(out=dd_qwc()[:rp, :],
                                in_=qwd[pt * 128:pt * 128 + rp, :]).then_inc(s_dl.h, 16)
                    s_dl.n += 16
                    M.mark(f"dd_qw_{pt}", s_dl)
                    ng = (rp + 15) // 16
                    for gg in range(ng):
                        g = 8 * pt + gg
                        npt = min(16, rp - gg * 16)
                        e.dma_start(out=dd_scb()[gg * 16:gg * 16 + npt, :],
                                    in_=ddsc_dram.ap()[g:g + 1, :].to_broadcast((npt, 512))).then_inc(s_aq.h, 16)
                        e.dma_start(out=dd_szqb()[gg * 16:gg * 16 + npt, :],
                                    in_=ddszq_dram.ap()[g:g + 1, :].to_broadcast((npt, 512))).then_inc(s_aq.h, 16)
                        s_aq.n += 32
                    M.mark(f"dd_bc_{pt}", s_aq)
                    e.wait_ge(s_dve.h, M.get(f"dd_scaled_{pt}"))
                    for kk in range(8):
                        s0 = 8 * 128 * pt + kk
                        dst = agd_in.ap()[s0: s0 + 8 * (rp - 1) + 1: 8, :]
                        e.dma_start(out=dst, in_=dd_pdb[kk]()[:rp, :]).then_inc(s_do.h, 16)
                        s_do.n += 16
                    M.mark(f"dd_out_{pt}", s_do)

                for tt in range(8):
                    src = bufC if tt % 2 == 0 else bufD
                    if tt == 0:
                        e.wait_ge(s_qo.h, M.get("dq_out_15"))
                    if tt >= 1:
                        e.wait_ge(s_pe.h, M.get(f"xtr_{tt-1}_31"))
                    e.wait_ge(s_x[tt % 2].h, M.get(f"xload_{tt}"))
                    e.wait_ge(s_dve.h, M.get(f"xstat_z_{tt}"))
                    e.activation(out=bufE[:], in_=src[:], func=AF.Square,
                                 accum_out=stat2[:]).then_inc(s_act.h, 1)
                    s_act.n += 1
                    M.mark(f"xsq_{tt}", s_act)
                    e.wait_ge(s_dve.h, M.get(f"xvar_{tt}"))
                    ss(e.activation(out=stat3[:], in_=stat3[:], func=AF.Ln, bias=ceps_sb[:]))
                    e.activation(out=stat3[:], in_=stat3[:], func=AF.Exp, scale=cmh_sb[:]).then_inc(s_act.h, 1)
                    s_act.n += 1
                    M.mark(f"xrstd_{tt}", s_act)
                    for q in range(32):
                        e.wait_ge(s_pe.h, M.get(f"xtr_{tt}_{q}"))
                        e.copy(out=xnT.ap()[:, q * 1024 + tt * 128: q * 1024 + tt * 128 + 128],
                               in_=psb(q % 2)).then_inc(s_act.h, 1)
                        s_act.n += 1
                        M.mark(f"xev_{tt}_{q}", s_act)
                M.mark("xn_all", s_act)

                for dp in range(4):
                    for hh in range(2):
                        k = dp * 2 + hh
                        e.wait_ge(s_pe.h, M.get(f"emb_mm_{k}"))
                        e.activation(out=bufC[:, dp * 1024 + hh * 512: dp * 1024 + hh * 512 + 512],
                                     in_=ps[k % 2][:], func=AF.Identity,
                                     bias=p_bin.ap()[:, dp:dp + 1]).then_inc(s_act.h, 1)
                        s_act.n += 1
                        M.mark(f"emb_ev_{k}", s_act)

                def ln_scalar(tag):
                    for dp in range(4):
                        e.wait_ge(s_dve.h, M.get(f"{tag}_sqrdy"))
                        e.activation(out=bufE[:, dp * 1024:(dp + 1) * 1024],
                                     in_=bufC[:, dp * 1024:(dp + 1) * 1024],
                                     func=AF.Square).then_inc(s_act.h, 1)
                        s_act.n += 1
                        M.mark(f"{tag}_sq_{dp}", s_act)
                    e.wait_ge(s_dve.h, M.get(f"{tag}_var"))
                    rv = bufF.ap()[:, 1024:2048]
                    ss(e.activation(out=rv, in_=rv, func=AF.Ln, bias=ceps_sb[:]))
                    e.activation(out=rv, in_=rv, func=AF.Exp, scale=cmh_sb[:]).then_inc(s_act.h, 1)
                    s_act.n += 1
                    M.mark(f"{tag}_rstd", s_act)

                for l in range(NL):
                    e.wait_ge(s_dve.h, M.get(f"p_r2_{l}"))
                    ss(e.activation(out=ab_sb[:, 8:12], in_=ab_sb[:, 8:12], func=AF.Sqrt))
                    e.activation(out=ab_sb[:, 12:16], in_=ab_sb[:, 8:12], func=AF.Tanh).then_inc(s_act.h, 1)
                    s_act.n += 1
                    M.mark(f"p_act_{l}", s_act)
                    ln_scalar(f"l1_{l}")
                    for dp in range(4):
                        e.wait_ge(s_dve.h, M.get(f"l1_{l}_app_{dp}"))
                        e.activation(out=bufD[:, dp * 1024:(dp + 1) * 1024],
                                     in_=bufD[:, dp * 1024:(dp + 1) * 1024],
                                     func=AF.Silu).then_inc(s_act.h, 1)
                        s_act.n += 1
                        M.mark(f"l1_{l}_silu_{dp}", s_act)
                    emit_g_scalar(G_SCHED[l][:6])
                    ln_scalar(f"l2_{l}")
                    for hh in range(2):
                        for m in range(16):
                            k = hh * 16 + m
                            e.wait_ge(s_pe.h, M.get(f"w1mm_{l}_{k}"))
                            e.activation(out=h1b()[:, m * 512:(m + 1) * 512],
                                         in_=ps[k % 2][:], func=AF.Silu,
                                         bias=p_b1.ap()[:, l * 16 + m: l * 16 + m + 1]).then_inc(s_act.h, 1)
                            s_act.n += 1
                            M.mark(f"w1ev_{l}_{k}", s_act)
                    emit_g_scalar(G_SCHED[l][6:])
                ln_scalar("lnl")
                for vt in range(32):
                    for hh in range(2):
                        k = vt * 2 + hh
                        e.wait_ge(s_pe.h, M.get(f"wo_mm_{k}"))
                        e.activation(out=htsl[vt % 2]()[:, hh * 512:hh * 512 + 512],
                                     in_=ps[k % 4][:], func=AF.Identity,
                                     bias=p_bout.ap()[:, vt:vt + 1]).then_inc(s_act.h, 1)
                        s_act.n += 1
                    M.mark(f"wo_ev_{vt}", s_act)
                    if vt >= 2:
                        e.wait_ge(s_aw[vt % 2].h, M.get(f"at_wr_{vt-2}"))
                    sa = s_aw[vt % 2]
                    e.dma_start(out=at_dram.ap()[vt], in_=htsl[vt % 2]()).then_inc(sa.h, 16)
                    sa.n += 16
                    M.mark(f"at_wr_{vt}", sa)
                M.mark("attn_sc_done", s_act)

                emit_g_scalar(G_POST)
                M.mark("h_all_w0", s_hw[0])
                M.mark("h_all_w1", s_hw[1])

                for hg in range(8):
                    for q in range(4):
                        k = hg * 4 + q
                        if k >= 2:
                            e.wait_ge(s_dve.h, M.get(f"dnev_{k-2}"))
                        sr = s_ar[k % 2]
                        e.dma_start(out=atsl[k % 2](), in_=at_dram.ap()[hg * 4 + q]).then_inc(sr.h, 16)
                        sr.n += 16
                        M.mark(f"at_rd_{k}", sr)

            @block.vector
            def _(e):
                vsn = [0]

                def vs(ins):
                    ins.then_inc(s_vs.h, 1)
                    vsn[0] += 1
                    e.wait_ge(s_vs.h, vsn[0])

                def emit_gating(pis):
                    for pi_ in pis:
                        upb = [6, 7] if pi_ % 2 == 0 else [2, 3]
                        if pi_ >= 2:
                            e.wait_ge(s_hw[pi_ % 2].h, M.get(f"hwr_{pi_-2}"))
                        last = None
                        for hh in range(2):
                            e.wait_ge(s_act.h, M.get(f"gsilu_{pi_}_{hh}"))
                            e.wait_ge(s_pe.h, M.get(f"umm_{pi_}_{hh}"))
                            last = e.tensor_mul(htsl[pi_ % 2]()[:, hh * 512:hh * 512 + 512],
                                                gs(pi_, hh), ps[upb[hh]][:])
                        last.then_inc(s_dve.h, 1)
                        s_dve.n += 1
                        M.mark(f"gating_{pi_}", s_dve)

                for jn, (i, side, qcol, wcols) in enumerate(jobs):
                    e.wait_ge(s_aq.h, M.get(f"dq_ld_{jn}"))
                    last = None
                    for kk in range(8):
                        last = e.tensor_scalar(out=dq_zqc()[:, kk:wcols:8], in0=dq_qzc()[:, :wcols // 8],
                                               scalar1=float(4 * kk), scalar2=15.0,
                                               op0=ALU.logical_shift_right, op1=ALU.bitwise_and)
                    vs(last)
                    e.tensor_copy(out=dq_zqf()[:, :wcols], in_=dq_zqc()[:, :wcols])
                    e.scalar_tensor_tensor(out=dq_szqs()[:, :wcols], in0=dq_zqf()[:, :wcols],
                                           scalar=1.0, in1=dq_scc()[:, :wcols],
                                           op0=ALU.add, op1=ALU.mult)
                    e.tensor_copy(out=dq_scs()[:, :wcols], in_=dq_scc()[:, :wcols]).then_inc(s_dve.h, 1)
                    s_dve.n += 1
                    M.mark(f"dq_src_{jn}", s_dve)
                    for m in range(4):
                        for hh in range(2):
                            kk = 2 * m + hh
                            e.tensor_scalar(out=dq_pp()[hh * 64:hh * 64 + 64, :wcols],
                                            in0=dq_qwc()[:, :wcols],
                                            scalar1=float(4 * kk), scalar2=15.0,
                                            op0=ALU.logical_shift_right, op1=ALU.bitwise_and)
                        e.tensor_copy(out=dq_ppb[m]()[:, :wcols], in_=dq_pp()[:, :wcols])
                    e.wait_ge(s_aq.h, M.get(f"dq_bc_{jn}"))
                    last = None
                    for m in range(4):
                        e.tensor_mul(dq_ppb[m]()[:, :wcols], dq_ppb[m]()[:, :wcols], dq_scb()[:, :wcols])
                    for m in range(4):
                        last = e.tensor_sub(dq_ppb[m]()[:, :wcols], dq_ppb[m]()[:, :wcols], dq_szqb()[:, :wcols])
                    last.then_inc(s_dve.h, 1)
                    s_dve.n += 1
                    M.mark(f"dq_scaled_{jn}", s_dve)
                    e.wait_ge(s_qo.h, M.get(f"dq_out_{jn}"))
                e.wait_ge(s_aq.h, M.get("dd_ld"))
                last = None
                for kk in range(8):
                    last = e.tensor_scalar(out=dd_zqd()[:, kk::8], in0=dd_qzd(),
                                           scalar1=float(4 * kk), scalar2=15.0,
                                           op0=ALU.logical_shift_right, op1=ALU.bitwise_and)
                vs(last)
                e.tensor_copy(out=dd_zqdf(), in_=dd_zqd())
                e.scalar_tensor_tensor(out=dd_szqd(), in0=dd_zqdf(), scalar=1.0,
                                       in1=dd_scd(), op0=ALU.add, op1=ALU.mult)
                e.tensor_copy(out=dd_scdb(), in_=dd_scd()).then_inc(s_dve.h, 1)
                s_dve.n += 1
                M.mark("dd_src", s_dve)
                for pt in range(11):
                    rp = min(128, 1376 - pt * 128)
                    M.mark(f"dd_free_{pt}", s_dve)
                    e.wait_ge(s_dl.h, M.get(f"dd_qw_{pt}"))
                    e.wait_ge(s_aq.h, M.get(f"dd_bc_{pt}"))
                    last = None
                    for kk in range(8):
                        vs(e.tensor_scalar(out=dd_pd()[:rp, :], in0=dd_qwc()[:rp, :],
                                           scalar1=float(4 * kk), scalar2=15.0,
                                           op0=ALU.logical_shift_right, op1=ALU.bitwise_and))
                        vs(e.tensor_copy(out=dd_pdb[kk]()[:rp, :], in_=dd_pd()[:rp, :]))
                        vs(e.tensor_mul(dd_pdb[kk]()[:rp, :], dd_pdb[kk]()[:rp, :], dd_scb()[:rp, :]))
                        last = e.tensor_sub(dd_pdb[kk]()[:rp, :], dd_pdb[kk]()[:rp, :], dd_szqb()[:rp, :])
                    last.then_inc(s_dve.h, 1)
                    s_dve.n += 1
                    M.mark(f"dd_scaled_{pt}", s_dve)
                    e.wait_ge(s_do.h, M.get(f"dd_out_{pt}"))

                for tt in range(8):
                    src = bufC if tt % 2 == 0 else bufD
                    e.memset(stat2[:], 0.0).then_inc(s_dve.h, 1)
                    s_dve.n += 1
                    M.mark(f"xstat_z_{tt}", s_dve)
                    e.wait_ge(s_x[tt % 2].h, M.get(f"xload_{tt}"))
                    vs(e.tensor_reduce(out=stat1[:], in_=src[:], axis=AX.X, op=ALU.add))
                    vs(e.tensor_scalar(out=stat1[:], in0=stat1[:], scalar1=1.0 / H,
                                       scalar2=None, op0=ALU.mult))
                    e.wait_ge(s_act.h, M.get(f"xsq_{tt}"))
                    vs(e.tensor_mul(stat3[:], stat1[:], stat1[:]))
                    e.scalar_tensor_tensor(out=stat3[:], in0=stat2[:], scalar=1.0 / H,
                                           in1=stat3[:], op0=ALU.mult,
                                           op1=ALU.subtract).then_inc(s_dve.h, 1)
                    s_dve.n += 1
                    M.mark(f"xvar_{tt}", s_dve)
                    e.wait_ge(s_act.h, M.get(f"xrstd_{tt}"))
                    if tt >= 1:
                        e.wait_ge(s_pe.h, M.get(f"xtr_{tt-1}_31"))
                    e.tensor_scalar(out=xnb(), in0=src[:], scalar1=stat1[:],
                                    scalar2=stat3[:], op0=ALU.subtract,
                                    op1=ALU.mult).then_inc(s_dve.h, 1)
                    s_dve.n += 1
                    M.mark(f"xn_done_{tt}", s_dve)

                e.wait_ge(s_do.h, M.get("dd_out_10"))
                for wc in range(8):
                    e.wait_ge(s_sq.h, M.get(f"winload_{wc}"))
                    e.tensor_copy(out=winb()[:, wc * 2048:(wc + 1) * 2048],
                                  in_=stg[wc % 2]()).then_inc(s_dve.h, 1)
                    s_dve.n += 1
                    M.mark(f"wincast_{wc}", s_dve)
                M.mark("win_ready", s_dve)

                def ln_vector(tag, src, dst_ap, wt, bt):
                    M.mark(f"{tag}_sqrdy", s_dve)
                    e.wait_ge(s_pe.h, M.get(f"{tag}_mm"))
                    m_sb = bufF.ap()[:, 0:1024]
                    var = bufF.ap()[:, 1024:2048]
                    e.tensor_copy(out=m_sb[:, 0:512], in_=ps[0][:])
                    e.tensor_copy(out=m_sb[:, 512:1024], in_=ps[1][:])
                    e.tensor_mul(var[:, 0:512], m_sb[:, 0:512], m_sb[:, 0:512])
                    e.tensor_mul(var[:, 512:1024], m_sb[:, 512:1024], m_sb[:, 512:1024])
                    e.tensor_sub(var[:, 0:512], ps[2][:], var[:, 0:512])
                    e.tensor_sub(var[:, 512:1024], ps[3][:], var[:, 512:1024]).then_inc(s_dve.h, 1)
                    s_dve.n += 1
                    M.mark(f"{tag}_var", s_dve)
                    e.wait_ge(s_act.h, M.get(f"{tag}_rstd"))
                    for dp in range(4):
                        blk = slice(dp * 1024, (dp + 1) * 1024)
                        e.tensor_sub(bufE[:, blk], src[:, blk], m_sb)
                        e.tensor_mul(bufE[:, blk], bufE[:, blk], var)
                        e.tensor_scalar(out=dst_ap[:, blk], in0=bufE[:, blk],
                                        scalar1=wt(dp), scalar2=bt(dp),
                                        op0=ALU.mult, op1=ALU.add).then_inc(s_dve.h, 1)
                        s_dve.n += 1
                        M.mark(f"{tag}_app_{dp}", s_dve)

                def cmul_ss(ar_ap, ai_ap, br_ap, bi_ap, or_ap, oi_ap):
                    e.tensor_mul(tmpa[:, 0:1], ar_ap, br_ap)
                    e.tensor_mul(tmpb[:, 0:1], ai_ap, bi_ap)
                    e.tensor_mul(oi_ap, ar_ap, bi_ap)
                    e.tensor_mul(tmpb[:, 1:2], ai_ap, br_ap)
                    e.tensor_sub(or_ap, tmpa[:, 0:1], tmpb[:, 0:1])
                    e.tensor_add(oi_ap, oi_ap, tmpb[:, 1:2])

                apsd = lambda dp, k: sqa_sb.ap()[:, dp * 11 + k: dp * 11 + k + 1]
                bpsd = lambda dp, k: sqb_sb.ap()[:, dp * 11 + k: dp * 11 + k + 1]
                SL_u = lambda dp: Sst2_u.ap()[:, dp * 32:(dp + 1) * 32]
                SL_v = lambda dp: Sst2_v.ap()[:, dp * 32:(dp + 1) * 32]
                Fall_u = lambda dp: Sst2_u.ap()[:, 128 + dp * 8:128 + dp * 8 + 8]
                Fall_v = lambda dp: Sst2_v.ap()[:, 128 + dp * 8:128 + dp * 8 + 8]
                P_u = lambda dp: Sst2_u.ap()[:, 160 + dp * 26:160 + dp * 26 + 26]

                def build_pow_table(outa, outb, n_total, aps, bps, k0, init_one):
                    if init_one:
                        e.memset(outa[:, 0:1], 1.0)
                        e.memset(outb[:, 0:1], 0.0)
                    else:
                        e.tensor_copy(out=outa[:, 0:1], in_=aps(0))
                        e.tensor_copy(out=outb[:, 0:1], in_=bps(0))
                    n = 1
                    k = k0
                    while n < n_total:
                        mlen = min(n, n_total - n)
                        e.tensor_scalar_mul(tmpa[:, 0:mlen], outb[:, 0:mlen], bps(k))
                        e.scalar_tensor_tensor(out=outa[:, n:n + mlen], in0=outa[:, 0:mlen],
                                               scalar=aps(k), in1=tmpa[:, 0:mlen],
                                               op0=ALU.mult, op1=ALU.subtract)
                        e.tensor_scalar_mul(tmpb[:, 0:mlen], outa[:, 0:mlen], bps(k))
                        e.scalar_tensor_tensor(out=outb[:, n:n + mlen], in0=outb[:, 0:mlen],
                                               scalar=aps(k), in1=tmpb[:, 0:mlen],
                                               op0=ALU.mult, op1=ALU.add)
                        n += mlen
                        k += 1

                for l in range(NL):
                    prl = p_pr.ap()[:, l * 4:(l + 1) * 4]
                    pil = p_pi.ap()[:, l * 4:(l + 1) * 4]
                    e.tensor_mul(ab_sb[:, 8:12], prl, prl)
                    vs(e.tensor_mul(tmpb[:, 0:4], pil, pil))
                    vs(e.tensor_add(ab_sb[:, 8:12], ab_sb[:, 8:12], tmpb[:, 0:4]))
                    e.tensor_scalar(out=ab_sb[:, 8:12], in0=ab_sb[:, 8:12],
                                    scalar1=1e-30, scalar2=None, op0=ALU.max).then_inc(s_dve.h, 1)
                    s_dve.n += 1
                    M.mark(f"p_r2_{l}", s_dve)
                    e.wait_ge(s_act.h, M.get(f"p_act_{l}"))
                    vs(e.reciprocal(tmpa[:, 0:4], ab_sb[:, 8:12]))
                    vs(e.tensor_mul(tmpa[:, 0:4], tmpa[:, 0:4], ab_sb[:, 12:16]))
                    e.tensor_mul(ab_sb[:, 0:4], prl, tmpa[:, 0:4])
                    vs(e.tensor_mul(ab_sb[:, 4:8], pil, tmpa[:, 0:4]))
                    for dp in range(4):
                        e.tensor_copy(out=sqa_sb[:, dp * 11:dp * 11 + 1], in_=ab_sb[:, dp:dp + 1])
                    for dp in range(4):
                        e.tensor_copy(out=sqb_sb[:, dp * 11:dp * 11 + 1], in_=ab_sb[:, 4 + dp:4 + dp + 1])
                    for k in range(10):
                        for dp in range(4):
                            e.tensor_mul(tmpa[:, dp:dp + 1], apsd(dp, k), apsd(dp, k))
                        for dp in range(4):
                            e.tensor_mul(tmpb[:, dp:dp + 1], bpsd(dp, k), bpsd(dp, k))
                        for dp in range(4):
                            e.tensor_mul(tmpc[:, dp:dp + 1], apsd(dp, k), bpsd(dp, k))
                        for dp in range(4):
                            e.tensor_sub(sqa_sb[:, dp * 11 + k + 1:dp * 11 + k + 2],
                                         tmpa[:, dp:dp + 1], tmpb[:, dp:dp + 1])
                        for dp in range(4):
                            e.tensor_add(sqb_sb[:, dp * 11 + k + 1:dp * 11 + k + 2],
                                         tmpc[:, dp:dp + 1], tmpc[:, dp:dp + 1])
                    ln_vector(f"l1_{l}", bufC, bufD.ap(), lambda dp: pc(p_l1w, l, dp),
                              lambda dp: pc(p_l1b, l, dp))
                    for dp in range(4):
                        e.wait_ge(s_act.h, M.get(f"l1_{l}_silu_{dp}"))
                    e.wait_ge(s_pl.h, M.get(f"vmemset_{l}"))
                    u3 = [bufD.ap()[:, dp * 1024:dp * 1024 + 1024]
                          .rearrange("p (j k) -> p j k", k=CSZ) for dp in range(4)]
                    v3 = [bufE.ap()[:, dp * 1024:dp * 1024 + 1024]
                          .rearrange("p (j k) -> p j k", k=CSZ) for dp in range(4)]
                    LN_ = lambda t, dp, a0, b0: t.ap()[:, dp * 32 + a0: dp * 32 + b0]
                    pend_g = list(G_SCHED[l])
                    for kk in range(1, CSZ):
                        if kk % 5 == 0 and pend_g:
                            emit_gating([pend_g.pop(0)])
                        for dp in range(4):
                            e.tensor_scalar_mul(LN_(tmpa, dp, 0, JCH), v3[dp][:, :, kk - 1], bpsd(dp, 0))
                        for dp in range(4):
                            e.scalar_tensor_tensor(out=LN_(tmpb, dp, 0, JCH), in0=u3[dp][:, :, kk - 1],
                                                   scalar=apsd(dp, 0), in1=LN_(tmpa, dp, 0, JCH),
                                                   op0=ALU.mult, op1=ALU.subtract)
                        for dp in range(4):
                            e.tensor_add(u3[dp][:, :, kk], u3[dp][:, :, kk], LN_(tmpb, dp, 0, JCH))
                        for dp in range(4):
                            e.tensor_scalar_mul(LN_(tmpc, dp, 0, JCH), u3[dp][:, :, kk - 1], bpsd(dp, 0))
                        for dp in range(4):
                            e.scalar_tensor_tensor(out=v3[dp][:, :, kk], in0=v3[dp][:, :, kk - 1],
                                                   scalar=apsd(dp, 0), in1=LN_(tmpc, dp, 0, JCH),
                                                   op0=ALU.mult, op1=ALU.add)
                    for dp in range(4):
                        e.memset(LN_(Sua, dp, 0, 1), 0.0)
                    for dp in range(4):
                        e.memset(LN_(Sva, dp, 0, 1), 0.0)
                    for dp in range(4):
                        e.tensor_copy(out=LN_(Sua, dp, 1, JCH), in_=u3[dp][:, 0:JCH - 1, CSZ - 1])
                    for dp in range(4):
                        e.tensor_copy(out=LN_(Sva, dp, 1, JCH), in_=v3[dp][:, 0:JCH - 1, CSZ - 1])
                    cur, oth = (Sua, Sva), (Sub, Svb)
                    s = 1
                    for pas in range(5):
                        cu, cv = cur
                        ou, ov = oth
                        for dp in range(4):
                            e.tensor_copy(out=LN_(ou, dp, 0, s), in_=LN_(cu, dp, 0, s))
                        for dp in range(4):
                            e.tensor_copy(out=LN_(ov, dp, 0, s), in_=LN_(cv, dp, 0, s))
                        for dp in range(4):
                            e.tensor_scalar_mul(LN_(tmpa, dp, 0, JCH - s), LN_(cv, dp, 0, JCH - s), bpsd(dp, 5 + pas))
                        for dp in range(4):
                            e.scalar_tensor_tensor(out=LN_(tmpb, dp, 0, JCH - s), in0=LN_(cu, dp, 0, JCH - s),
                                                   scalar=apsd(dp, 5 + pas), in1=LN_(tmpa, dp, 0, JCH - s),
                                                   op0=ALU.mult, op1=ALU.subtract)
                        for dp in range(4):
                            e.tensor_add(LN_(ou, dp, s, JCH), LN_(cu, dp, s, JCH), LN_(tmpb, dp, 0, JCH - s))
                        for dp in range(4):
                            e.tensor_scalar_mul(LN_(tmpc, dp, 0, JCH - s), LN_(cu, dp, 0, JCH - s), bpsd(dp, 5 + pas))
                        for dp in range(4):
                            e.scalar_tensor_tensor(out=LN_(tmpd, dp, 0, JCH - s), in0=LN_(cv, dp, 0, JCH - s),
                                                   scalar=apsd(dp, 5 + pas), in1=LN_(tmpc, dp, 0, JCH - s),
                                                   op0=ALU.mult, op1=ALU.add)
                        for dp in range(4):
                            e.tensor_add(LN_(ov, dp, s, JCH), LN_(cv, dp, s, JCH), LN_(tmpd, dp, 0, JCH - s))
                        cur, oth = oth, cur
                        s *= 2
                    Su, Sv = cur
                    for dp in range(4):
                        e.tensor_copy(out=SL_u(dp), in_=LN_(Su, dp, 0, JCH))
                    for dp in range(4):
                        e.tensor_copy(out=SL_v(dp), in_=LN_(Sv, dp, 0, JCH))
                    for dp in range(4):
                        e.tensor_scalar_mul(tmpa[:, dp:dp + 1], LN_(Sv, dp, JCH - 1, JCH), bpsd(dp, 5))
                    for dp in range(4):
                        e.scalar_tensor_tensor(out=tmpb[:, dp:dp + 1], in0=LN_(Su, dp, JCH - 1, JCH),
                                               scalar=apsd(dp, 5), in1=tmpa[:, dp:dp + 1],
                                               op0=ALU.mult, op1=ALU.subtract)
                    for dp in range(4):
                        e.tensor_add(fco[:, 2 * dp:2 * dp + 1], tmpb[:, dp:dp + 1],
                                     u3[dp][:, JCH - 1, CSZ - 1:CSZ])
                    for dp in range(4):
                        e.tensor_scalar_mul(tmpc[:, dp:dp + 1], LN_(Su, dp, JCH - 1, JCH), bpsd(dp, 5))
                    for dp in range(4):
                        e.scalar_tensor_tensor(out=tmpd[:, dp:dp + 1], in0=LN_(Sv, dp, JCH - 1, JCH),
                                               scalar=apsd(dp, 5), in1=tmpc[:, dp:dp + 1],
                                               op0=ALU.mult, op1=ALU.add)
                    last = None
                    for dp in range(4):
                        last = e.tensor_add(fco[:, 2 * dp + 1:2 * dp + 2], tmpd[:, dp:dp + 1],
                                            v3[dp][:, JCH - 1, CSZ - 1:CSZ])
                    last.then_inc(s_dve.h, 1)
                    s_dve.n += 1
                    M.mark(f"F_ready_{l}", s_dve)
                    e.wait_ge(s_pe.h, M.get(f"Ftr_{l}"))
                    for dp in range(4):
                        e.tensor_copy(out=Fall_u(dp), in_=ps[0].ap()[:, dp * 8:dp * 8 + 8])
                    for dp in range(4):
                        e.tensor_copy(out=Fall_v(dp), in_=ps[1].ap()[:, dp * 8:dp * 8 + 8])
                    e.memset(tmpa[:, 0:1], 0.0).then_inc(s_dve.h, 1)
                    s_dve.n += 1
                    M.mark(f"Fcp_{l}", s_dve)
                    sall_u = lambda dp, k: sall.ap()[:, dp * 16 + k: dp * 16 + k + 1]
                    sall_v = lambda dp, k: sall.ap()[:, dp * 16 + 8 + k: dp * 16 + 8 + k + 1]
                    for dp in range(4):
                        e.tensor_copy(out=sall_u(dp, 0), in_=pc(p_lcr, l, dp))
                    for dp in range(4):
                        e.tensor_copy(out=sall_v(dp, 0), in_=pc(p_lci, l, dp))
                    for kk in range(7):
                        for dp in range(4):
                            e.tensor_scalar_mul(tmpa[:, dp:dp + 1], sall_v(dp, kk), bpsd(dp, 10))
                        for dp in range(4):
                            e.scalar_tensor_tensor(out=tmpb[:, dp:dp + 1], in0=sall_u(dp, kk),
                                                   scalar=apsd(dp, 10), in1=tmpa[:, dp:dp + 1],
                                                   op0=ALU.mult, op1=ALU.subtract)
                        for dp in range(4):
                            e.tensor_add(sall_u(dp, kk + 1), tmpb[:, dp:dp + 1],
                                         Fall_u(dp)[:, kk:kk + 1])
                        for dp in range(4):
                            e.tensor_scalar_mul(tmpc[:, dp:dp + 1], sall_u(dp, kk), bpsd(dp, 10))
                        for dp in range(4):
                            e.scalar_tensor_tensor(out=tmpd[:, dp:dp + 1], in0=sall_v(dp, kk),
                                                   scalar=apsd(dp, 10), in1=tmpc[:, dp:dp + 1],
                                                   op0=ALU.mult, op1=ALU.add)
                        for dp in range(4):
                            e.tensor_add(sall_v(dp, kk + 1), tmpd[:, dp:dp + 1],
                                         Fall_v(dp)[:, kk:kk + 1])
                    for dp in range(4):
                        e.tensor_mul(tmpa[:, dp * 8:dp * 8 + 8], sall.ap()[:, dp * 16:dp * 16 + 8], selb_sb[:])
                    for dp in range(4):
                        e.tensor_reduce(out=fco[:, 8 + 2 * dp:9 + 2 * dp], in_=tmpa[:, dp * 8:dp * 8 + 8],
                                        axis=AX.X, op=ALU.add)
                    for dp in range(4):
                        e.tensor_mul(tmpb[:, dp * 8:dp * 8 + 8], sall.ap()[:, dp * 16 + 8:dp * 16 + 16], selb_sb[:])
                    for dp in range(4):
                        e.tensor_reduce(out=fco[:, 9 + 2 * dp:10 + 2 * dp], in_=tmpb[:, dp * 8:dp * 8 + 8],
                                        axis=AX.X, op=ALU.add)
                    def pow4(outa, outb, n_total, kbase, init_one):
                        if init_one:
                            for dp in range(4):
                                e.memset(outa.ap()[:, dp * 32:dp * 32 + 1], 1.0)
                            for dp in range(4):
                                e.memset(outb.ap()[:, dp * 32:dp * 32 + 1], 0.0)
                        else:
                            for dp in range(4):
                                e.tensor_copy(out=outa.ap()[:, dp * 32:dp * 32 + 1], in_=apsd(dp, kbase))
                            for dp in range(4):
                                e.tensor_copy(out=outb.ap()[:, dp * 32:dp * 32 + 1], in_=bpsd(dp, kbase))
                        n, k = 1, kbase
                        while n < n_total:
                            mlen = min(n, n_total - n)
                            for dp in range(4):
                                e.tensor_scalar_mul(tmpa[:, dp * 32:dp * 32 + mlen],
                                                    outb.ap()[:, dp * 32:dp * 32 + mlen], bpsd(dp, k))
                            for dp in range(4):
                                e.scalar_tensor_tensor(out=outa.ap()[:, dp * 32 + n:dp * 32 + n + mlen],
                                                       in0=outa.ap()[:, dp * 32:dp * 32 + mlen],
                                                       scalar=apsd(dp, k), in1=tmpa[:, dp * 32:dp * 32 + mlen],
                                                       op0=ALU.mult, op1=ALU.subtract)
                            for dp in range(4):
                                e.tensor_scalar_mul(tmpb[:, dp * 32:dp * 32 + mlen],
                                                    outa.ap()[:, dp * 32:dp * 32 + mlen], bpsd(dp, k))
                            for dp in range(4):
                                e.scalar_tensor_tensor(out=outb.ap()[:, dp * 32 + n:dp * 32 + n + mlen],
                                                       in0=outb.ap()[:, dp * 32:dp * 32 + mlen],
                                                       scalar=apsd(dp, k), in1=tmpb[:, dp * 32:dp * 32 + mlen],
                                                       op0=ALU.mult, op1=ALU.add)
                            n += mlen
                            k += 1
                    pow4(P32a, P32b, CSZ, 0, init_one=False)
                    pow4(Qpa, Qpb, JCH, 5, init_one=True)
                    sc_re = lambda dp: fco.ap()[:, 8 + 2 * dp:9 + 2 * dp]
                    sc_im = lambda dp: fco.ap()[:, 9 + 2 * dp:10 + 2 * dp]
                    for dp in range(4):
                        e.tensor_scalar_mul(LN_(tmpa, dp, 0, 32), Qpa.ap()[:, dp * 32:dp * 32 + 32], sc_re(dp))
                    for dp in range(4):
                        e.tensor_scalar_mul(LN_(tmpb, dp, 0, 32), Qpb.ap()[:, dp * 32:dp * 32 + 32], sc_im(dp))
                    for dp in range(4):
                        e.tensor_sub(LN_(tmpa, dp, 0, 32), LN_(tmpa, dp, 0, 32), LN_(tmpb, dp, 0, 32))
                    for dp in range(4):
                        e.tensor_add(LN_(Sua, dp, 0, 32), SL_u(dp), LN_(tmpa, dp, 0, 32))
                    for dp in range(4):
                        e.tensor_scalar_mul(LN_(tmpc, dp, 0, 32), Qpb.ap()[:, dp * 32:dp * 32 + 32], sc_re(dp))
                    for dp in range(4):
                        e.tensor_scalar_mul(LN_(tmpd, dp, 0, 32), Qpa.ap()[:, dp * 32:dp * 32 + 32], sc_im(dp))
                    for dp in range(4):
                        e.tensor_add(LN_(tmpc, dp, 0, 32), LN_(tmpc, dp, 0, 32), LN_(tmpd, dp, 0, 32))
                    for dp in range(4):
                        e.tensor_add(LN_(Sva, dp, 0, 32), SL_v(dp), LN_(tmpc, dp, 0, 32))
                    last = None
                    for dp in range(4):
                        blk0 = dp * 1024
                        Sbu_ = bufE.ap()[:, 2048:3072].rearrange("p (j k) -> p j k", k=CSZ)
                        Sbv_ = bufE.ap()[:, 3072:4096].rearrange("p (j k) -> p j k", k=CSZ)
                        e.tensor_copy(out=Sbu_, in_=Sua.ap()[:, dp * 32:dp * 32 + 32, None]
                                      .broadcast_to([128, JCH, CSZ]))
                        e.tensor_copy(out=Sbv_, in_=Sva.ap()[:, dp * 32:dp * 32 + 32, None]
                                      .broadcast_to([128, JCH, CSZ]))
                        pu = bufE.ap()[:, 0:1024].rearrange("p (j k) -> p j k", k=CSZ)
                        pv = bufE.ap()[:, 1024:2048].rearrange("p (j k) -> p j k", k=CSZ)
                        Pab = P32a.ap()[:, None, dp * 32:dp * 32 + 32].broadcast_to([128, JCH, CSZ])
                        Pbb = P32b.ap()[:, None, dp * 32:dp * 32 + 32].broadcast_to([128, JCH, CSZ])
                        e.tensor_tensor(out=pu, in0=Sbu_, in1=Pab, op=ALU.mult)
                        e.tensor_tensor(out=pv, in0=Sbv_, in1=Pbb, op=ALU.mult)
                        ub = bufD.ap()[:, blk0:blk0 + 1024]
                        e.tensor_add(ub, ub, bufE.ap()[:, 0:1024])
                        e.tensor_sub(ub, ub, bufE.ap()[:, 1024:2048])
                        last = e.scalar_tensor_tensor(out=bufC[:, blk0:blk0 + 1024],
                                                      in0=ub, scalar=pc(p_scl, l, dp),
                                                      in1=bufC[:, blk0:blk0 + 1024],
                                                      op0=ALU.mult, op1=ALU.add)
                    last.then_inc(s_dve.h, 1)
                    s_dve.n += 1
                    for dp in range(4):
                        M.mark(f"scan_done_{l}_{dp}", s_dve)
                    ln_vector(f"l2_{l}", bufC, y2b(), lambda dp: pc(p_l2w, l, dp),
                              lambda dp: pc(p_l2b, l, dp))
                    for wc in range(4):
                        M.mark(f"w1cast_{l}_{wc}_free", s_dve)
                        e.wait_ge(s_sq.h, M.get(f"w1load_{l}_{wc}"))
                        e.tensor_copy(out=w1b()[:, wc * 2048:(wc + 1) * 2048],
                                      in_=stg[wc % 2]()).then_inc(s_dve.h, 1)
                        s_dve.n += 1
                        M.mark(f"w1cast_{l}_{wc}", s_dve)
                    for wc in range(4):
                        M.mark(f"w2cast_{l}_{wc}_free", s_dve)
                        e.wait_ge(s_sq.h, M.get(f"w2load_{l}_{wc}"))
                        e.tensor_copy(out=w2b()[:, wc * 2048:(wc + 1) * 2048],
                                      in_=stg[wc % 2]()).then_inc(s_dve.h, 1)
                        s_dve.n += 1
                        M.mark(f"w2cast_{l}_{wc}", s_dve)
                    for hh in range(2):
                        for dp in range(4):
                            k = hh * 4 + dp
                            e.wait_ge(s_pe.h, M.get(f"w2mm_{l}_{k}"))
                            blk = slice(dp * 1024 + hh * 512, dp * 1024 + hh * 512 + 512)
                            e.tensor_scalar(out=bufE[:, 0:512], in0=ps[dp][:],
                                            scalar1=pc(p_b2, l, dp), scalar2=None,
                                            op0=ALU.add)
                            e.tensor_add(bufC[:, blk], bufC[:, blk],
                                         bufE[:, 0:512]).then_inc(s_dve.h, 1)
                            s_dve.n += 1
                            M.mark(f"w2ev_{l}_{k}", s_dve)
                    emit_gating(pend_g)

                ln_vector("lnl", bufC, y2b(), lambda dp: p_lnlw.ap()[:, dp:dp + 1],
                          lambda dp: p_lnlb.ap()[:, dp:dp + 1])
                for wc in range(8):
                    M.mark(f"woutcast_{wc}_free", s_dve)
                    e.wait_ge(s_sq.h, M.get(f"woutload_{wc}"))
                    e.tensor_copy(out=woutb()[:, wc * 2048:(wc + 1) * 2048],
                                  in_=stg[wc % 2]()).then_inc(s_dve.h, 1)
                    s_dve.n += 1
                    M.mark(f"woutcast_{wc}", s_dve)
                e.wait_ge(s_act.h, M.get("attn_sc_done"))
                e.wait_ge(s_aw[0].h, M.get("at_wr_30"))
                e.wait_ge(s_aw[1].h, M.get("at_wr_31"))
                e.memset(tmpa[:, 0:1], 0.0).then_inc(s_dve.h, 1)
                s_dve.n += 1
                M.mark("attn_done", s_dve)

                emit_gating(G_POST)
                M.mark("gating_done", s_dve)

                for hg in range(8):
                    for q in range(4):
                        k = hg * 4 + q
                        e.wait_ge(s_pe.h, M.get(f"dn_acc_{hg}"))
                        e.wait_ge(s_ar[k % 2].h, M.get(f"at_rd_{k}"))
                        ht = hg * 4 + q
                        last = None
                        for hh in range(2):
                            last = e.tensor_add(
                                brsum.ap()[:, ht * 1024 + hh * 512: ht * 1024 + hh * 512 + 512],
                                atsl[k % 2]()[:, hh * 512:hh * 512 + 512],
                                ps[q * 2 + hh][:])
                        last.then_inc(s_dve.h, 1)
                        s_dve.n += 1
                        M.mark(f"dnev_{k}", s_dve)
                M.mark("dn_evac_done", s_dve)

                for tt in range(8):
                    e.wait_ge(s_ox.h, M.get(f"oxload_{tt}"))
                    if tt >= 2:
                        e.wait_ge(s_os[tt % 2].h, M.get(f"ostore_{tt-2}"))
                    dstbuf = bufD if tt % 2 == 0 else bufE
                    for q in range(32):
                        e.wait_ge(s_pe.h, M.get(f"otr_{tt}_{q}"))
                        e.tensor_add(dstbuf[:, q * 128:(q + 1) * 128],
                                     bufC[:, q * 128:(q + 1) * 128],
                                     psb(q % 2)).then_inc(s_dve.h, 1)
                        s_dve.n += 1
                        M.mark(f"otrd_{tt}_{q}", s_dve)
                    M.mark(f"oadd_{tt}", s_dve)

            @block.tensor
            def _(e):
                def emit_g_tiles(pis):
                    for pi_ in pis:
                        k0 = 2 * pi_
                        e.wait_ge(s_gs[k0 % 2].h, M.get(f"gu_slab_{k0}"))
                        e.wait_ge(s_gs[(k0 + 1) % 2].h, M.get(f"gu_slab_{k0 + 1}"))
                        if pi_ >= 1:
                            e.wait_ge(s_act.h, M.get(f"gsilu_{pi_-1}_1"))
                        for hh in range(2):
                            for q in range(32):
                                mm = e.matmul(ps[4 + hh][:],
                                              gslab[k0 % 2]()[:, q * 128:(q + 1) * 128],
                                              xnT.ap()[:, q * 1024 + hh * 512: q * 1024 + hh * 512 + 512],
                                              start=(q == 0), stop=(q == 31))
                            mm.then_inc(s_pe.h, 1)
                            s_pe.n += 1
                            M.mark(f"gmm_{pi_}_{hh}", s_pe)
                        upb = [6, 7] if pi_ % 2 == 0 else [2, 3]
                        if pi_ >= 2:
                            e.wait_ge(s_dve.h, M.get(f"gating_{pi_-2}"))
                        for hh in range(2):
                            for q in range(32):
                                mm = e.matmul(ps[upb[hh]][:],
                                              gslab[(k0 + 1) % 2]()[:, q * 128:(q + 1) * 128],
                                              xnT.ap()[:, q * 1024 + hh * 512: q * 1024 + hh * 512 + 512],
                                              start=(q == 0), stop=(q == 31))
                            mm.then_inc(s_pe.h, 1)
                            s_pe.n += 1
                            M.mark(f"umm_{pi_}_{hh}", s_pe)
                        M.mark(f"gu_mm_{k0}", s_pe)
                        M.mark(f"gu_mm_{k0 + 1}", s_pe)

                e.wait_ge(s_sq.h, M.get("smalls"))
                for tt in range(8):
                    e.wait_ge(s_dve.h, M.get(f"xn_done_{tt}"))
                    for q in range(32):
                        fl = tt * 32 + q
                        if fl >= 2:
                            pq = fl - 2
                            e.wait_ge(s_act.h, M.get(f"xev_{pq // 32}_{pq % 32}"))
                        e.transpose(psb(q % 2), xnb()[:, q * 128:(q + 1) * 128],
                                    idb_sb[:]).then_inc(s_pe.h, 1)
                        s_pe.n += 1
                        M.mark(f"xtr_{tt}_{q}", s_pe)

                e.wait_ge(s_dve.h, M.get("win_ready"))
                e.wait_ge(s_act.h, M.get("xn_all"))
                for dp in range(4):
                    for hh in range(2):
                        k = dp * 2 + hh
                        if k >= 2:
                            e.wait_ge(s_act.h, M.get(f"emb_ev_{k-2}"))
                        for q in range(32):
                            mm = e.matmul(ps[k % 2][:],
                                          winb()[:, q * 512 + dp * 128: q * 512 + (dp + 1) * 128],
                                          xnT.ap()[:, q * 1024 + hh * 512: q * 1024 + hh * 512 + 512],
                                          start=(q == 0), stop=(q == 31))
                        mm.then_inc(s_pe.h, 1)
                        s_pe.n += 1
                        M.mark(f"emb_mm_{k}", s_pe)

                def ln_tensor(tag, src):
                    for dp in range(4):
                        e.wait_ge(s_act.h, M.get(f"{tag}_sq_{dp}"))
                        for hh in range(2):
                            e.matmul(ps[0 + hh][:], ones_sb[:],
                                     src[:, dp * 1024 + hh * 512: dp * 1024 + hh * 512 + 512],
                                     start=(dp == 0), stop=(dp == 3))
                            mm = e.matmul(ps[2 + hh][:], ones_sb[:],
                                          bufE[:, dp * 1024 + hh * 512: dp * 1024 + hh * 512 + 512],
                                          start=(dp == 0), stop=(dp == 3))
                    mm.then_inc(s_pe.h, 1)
                    s_pe.n += 1
                    M.mark(f"{tag}_mm", s_pe)

                for l in range(NL):
                    ln_tensor(f"l1_{l}", bufC)
                    e.wait_ge(s_dve.h, M.get(f"l1_{l}_app_3"))
                    emit_g_tiles(G_SCHED[l][:6])
                    e.wait_ge(s_fg.h, M.get(f"Fgather_{l}"))
                    e.wait_ge(s_dve.h, M.get(f"F_ready_{l}"))
                    for dp in range(4):
                        e.transpose(ps[0].ap()[:, dp * 8:dp * 8 + 8],
                                    agf_sb[0:8, dp * 256:dp * 256 + 128],
                                    idf_sb[:8, :8]).then_inc(s_pe.h, 1)
                        s_pe.n += 1
                        e.transpose(ps[1].ap()[:, dp * 8:dp * 8 + 8],
                                    agf_sb[0:8, dp * 256 + 128:dp * 256 + 256],
                                    idf_sb[:8, :8]).then_inc(s_pe.h, 1)
                        s_pe.n += 1
                    M.mark(f"Ftr_{l}", s_pe)
                    for dp in range(4):
                        e.wait_ge(s_dve.h, M.get(f"scan_done_{l}_{dp}"))
                    ln_tensor(f"l2_{l}", bufC)
                    for hh in range(2):
                        for m in range(16):
                            k = hh * 16 + m
                            e.wait_ge(s_dve.h, M.get(f"l2_{l}_app_3"))
                            e.wait_ge(s_dve.h, M.get(f"w1cast_{l}_3"))
                            if k >= 2:
                                e.wait_ge(s_act.h, M.get(f"w1ev_{l}_{k-2}"))
                            for q in range(4):
                                mm = e.matmul(ps[k % 2][:],
                                              w1b()[:, q * DFF + m * 128: q * DFF + (m + 1) * 128],
                                              y2b()[:, q * 1024 + hh * 512: q * 1024 + hh * 512 + 512],
                                              start=(q == 0), stop=(q == 3))
                            mm.then_inc(s_pe.h, 1)
                            s_pe.n += 1
                            M.mark(f"w1mm_{l}_{k}", s_pe)
                        for dp in range(4):
                            kk2 = hh * 4 + dp
                            e.wait_ge(s_dve.h, M.get(f"w2cast_{l}_3"))
                            e.wait_ge(s_act.h, M.get(f"w1ev_{l}_{hh * 16 + 15}"))
                            if l > 0 or hh > 0:
                                e.wait_ge(s_dve.h, M.get(f"w2ev_{l if hh > 0 else l - 1}_{dp if hh > 0 else 4 + dp}"))
                            for m in range(16):
                                mm = e.matmul(ps[dp][:],
                                              w2b()[:, m * 512 + dp * 128: m * 512 + (dp + 1) * 128],
                                              h1b()[:, m * 512:(m + 1) * 512],
                                              start=(m == 0), stop=(m == 15))
                            mm.then_inc(s_pe.h, 1)
                            s_pe.n += 1
                            M.mark(f"w2mm_{l}_{kk2}", s_pe)
                    emit_g_tiles(G_SCHED[l][6:])

                ln_tensor("lnl", bufC)
                for vt in range(32):
                    for hh in range(2):
                        k = vt * 2 + hh
                        e.wait_ge(s_dve.h, M.get("woutcast_7"))
                        e.wait_ge(s_dve.h, M.get("lnl_app_3"))
                        if k >= 4:
                            e.wait_ge(s_act.h, M.get(f"wo_ev_{(k - 4) // 2}"))
                        for q in range(4):
                            mm = e.matmul(ps[k % 4][:],
                                          woutb()[:, q * 4096 + vt * 128: q * 4096 + (vt + 1) * 128],
                                          y2b()[:, q * 1024 + hh * 512: q * 1024 + hh * 512 + 512],
                                          start=(q == 0), stop=(q == 3))
                        mm.then_inc(s_pe.h, 1)
                        s_pe.n += 1
                        M.mark(f"wo_mm_{k}", s_pe)

                e.wait_ge(s_act.h, M.get("wo_ev_31"))
                emit_g_tiles(G_POST)

                for hg in range(8):
                    if hg >= 1:
                        for q in range(4):
                            e.wait_ge(s_dve.h, M.get(f"dnev_{(hg - 1) * 4 + q}"))
                    for ft in range(NT_G):
                        k = hg * NT_G + ft
                        e.wait_ge(s_ds[k % 4].h, M.get(f"dn_slab_{k}"))
                        for q in range(4):
                            for hh in range(2):
                                mm = e.matmul(ps[q * 2 + hh][:],
                                              wdsl[k % 4]()[:, q * 128:(q + 1) * 128],
                                              hsl[k % 4]()[:, hh * 512:hh * 512 + 512],
                                              start=(ft == 0), stop=(ft == NT_G - 1))
                        mm.then_inc(s_pe.h, 1)
                        s_pe.n += 1
                        M.mark(f"dn_mm_{k}", s_pe)
                    M.mark(f"dn_acc_{hg}", s_pe)

                for tt in range(8):
                    for q in range(32):
                        fl = tt * 32 + q
                        if fl == 0:
                            e.wait_ge(s_dve.h, M.get("dn_evac_done"))
                        if fl >= 2:
                            e.wait_ge(s_dve.h, M.get(f"otrd_{(fl - 2) // 32}_{(fl - 2) % 32}"))
                        e.transpose(psb(q % 2),
                                    brsum.ap()[:, q * 1024 + tt * 128: q * 1024 + tt * 128 + 128],
                                    idb_sb[:]).then_inc(s_pe.h, 1)
                        s_pe.n += 1
                        M.mark(f"otr_{tt}_{q}", s_pe)

            @block.gpsimd
            def _(e):
                for entry in cc_order:
                    if entry[0] == "gu":
                        i = entry[1]
                        e.wait_ge(s_qo.h, M.get(f"dq_out_{2 * i + 1}"))
                        e.collective_compute(
                            "AllGather", ALU.bypass,
                            replica_groups=[list(range(CORES))],
                            ins=[aggu_in[i].ap().opt()],
                            outs=[aggu_out[i].ap().opt()]).then_inc(s_cc.h, 1)
                        s_cc.n += 1
                    elif entry[0] == "scan":
                        l = entry[1]
                        e.wait_ge(s_dve.h, M.get(f"l1_{l}_app_3"))
                        e.memset(bufE[:], 0.0).then_inc(s_pl.h, 1)
                        s_pl.n += 1
                        M.mark(f"vmemset_{l}", s_pl)
                        e.wait_ge(s_dve.h, M.get(f"F_ready_{l}"))
                        for dp in range(4):
                            e.dma_start(out=agf_in[l].ap()[0, dp * 256:dp * 256 + 128][:, None],
                                        in_=fco[:, 2 * dp:2 * dp + 1]).then_inc(s_fw.h, 16)
                            e.dma_start(out=agf_in[l].ap()[0, dp * 256 + 128:dp * 256 + 256][:, None],
                                        in_=fco[:, 2 * dp + 1:2 * dp + 2]).then_inc(s_fw.h, 16)
                            s_fw.n += 32
                        e.wait_ge(s_fw.h, s_fw.n)
                        e.collective_compute(
                            "AllGather", ALU.bypass,
                            replica_groups=[list(range(CORES))],
                            ins=[agf_in[l].ap().opt()],
                            outs=[agf_out[l].ap().opt()]).then_inc(s_cc.h, 1)
                        s_cc.n += 1
                        e.wait_ge(s_cc.h, cc_idx[("scan", l)])
                        e.dma_start(out=agf_sb, in_=agf_out[l].ap()).then_inc(s_fg.h, 16)
                        s_fg.n += 16
                        M.mark(f"Fgather_{l}", s_fg)
                    else:
                        e.wait_ge(s_do.h, M.get("dd_out_10"))
                        e.collective_compute(
                            "AllGather", ALU.bypass,
                            replica_groups=[list(range(CORES))],
                            ins=[agd_in.ap().opt()],
                            outs=[agd_out.ap().opt()]).then_inc(s_cc.h, 1)
                        s_cc.n += 1

    return nc, M


def prep_inputs(inputs):
    f32 = np.float32

    def perm_d(a):
        return np.ascontiguousarray(
            np.asarray(a).reshape(NL, 4, 128).transpose(2, 0, 1).reshape(128, NL * 4)).astype(f32)

    x = np.asarray(inputs["L0_attn_x"]).astype(f32)
    common = {
        "winT": np.ascontiguousarray(np.asarray(inputs["L0_attn_Win"]).T).astype(f32),
        "w1T": np.ascontiguousarray(np.asarray(inputs["L0_attn_W1"]).transpose(0, 2, 1)).astype(f32),
        "w2T": np.ascontiguousarray(np.asarray(inputs["L0_attn_W2"]).transpose(0, 2, 1)).astype(f32),
        "woutT": np.ascontiguousarray(np.asarray(inputs["L0_attn_Wout"]).T).astype(f32),
        "bin_p": np.ascontiguousarray(np.asarray(inputs["L0_attn_bin"]).reshape(4, 128).T).astype(f32),
        "b1_p": np.ascontiguousarray(np.asarray(inputs["L0_attn_b1"]).reshape(NL, 16, 128)
                                     .transpose(2, 0, 1).reshape(128, NL * 16)).astype(f32),
        "b2_p": perm_d(inputs["L0_attn_b2"]),
        "bout_p": np.ascontiguousarray(np.asarray(inputs["L0_attn_bout"]).reshape(32, 128).T).astype(f32),
        "ln1w_p": perm_d(inputs["L0_attn_ln1_w"]),
        "ln1b_p": perm_d(inputs["L0_attn_ln1_b"]),
        "ln2w_p": perm_d(inputs["L0_attn_ln2_w"]),
        "ln2b_p": perm_d(inputs["L0_attn_ln2_b"]),
        "scl_p": perm_d(inputs["L0_attn_sc_lin"]),
        "pr_p": perm_d(inputs["L0_attn_phz_re"]),
        "pi_p": perm_d(inputs["L0_attn_phz_im"]),
        "lcr_p": perm_d(inputs["L0_attn_lc_re"]),
        "lci_p": perm_d(inputs["L0_attn_lc_im"]),
        "lnlw_p": np.ascontiguousarray(np.asarray(inputs["L0_attn_lnl_w"]).reshape(4, 128).T).astype(f32),
        "lnlb_p": np.ascontiguousarray(np.asarray(inputs["L0_attn_lnl_b"]).reshape(4, 128).T).astype(f32),
        "idb": np.eye(128, dtype=ml_dtypes.bfloat16),
        "idf": np.eye(128, dtype=np.float32),
        "ones512": np.full((128, 128), 1.0 / 512, dtype=f32),
        "ceps": np.full((128, 1), 1e-5, dtype=f32),
        "cmh": np.full((128, 1), -0.5, dtype=f32),
    }
    qwgu = np.asarray(inputs["L0_ffn_qweight_gu"])
    scgu = np.asarray(inputs["L0_ffn_scales_gu"])
    qzgu = np.asarray(inputs["L0_ffn_qzeros_gu"])
    qwd = np.asarray(inputs["L0_ffn_qweight_down"])
    scdo = np.asarray(inputs["L0_ffn_scales_down"])
    qzdo = np.asarray(inputs["L0_ffn_qzeros_down"])

    in_maps = []
    for c in range(CORES):
        m = dict(common)
        m["x_tok"] = np.ascontiguousarray(x[0, c * TOK:(c + 1) * TOK, :])
        m["qwgu"] = np.ascontiguousarray(qwgu[c * 64:(c + 1) * 64, :])
        m["scgu"] = np.ascontiguousarray(scgu[c * 4:(c + 1) * 4, :]).astype(f32)
        m["qzgu"] = np.ascontiguousarray(qzgu[c * 4:(c + 1) * 4, :])
        m["qwd"] = np.ascontiguousarray(qwd[:, c * 512:(c + 1) * 512])
        m["scd"] = np.ascontiguousarray(scdo[:, c * 512:(c + 1) * 512]).astype(f32)
        m["qzd"] = np.ascontiguousarray(qzdo[:, c * 64:(c + 1) * 64])
        s = np.zeros((1, 8), f32)
        s[0, c] = 1.0
        m["sel"] = s
        in_maps.append(m)
    return in_maps


def kernel(**inputs):
    in_maps = prep_inputs(inputs)
    _, marks = build(None)
    nc, _ = build(marks)
    res = run_bass_kernel_spmd(nc, in_maps, core_ids=list(range(CORES)))
    out = np.empty((1, CORES * TOK, H), np.float32)
    for c in range(CORES):
        out[0, c * TOK:(c + 1) * TOK] = res.results[c]["out"]
    return out


# version 09
# speedup vs baseline: 1.0442x; 1.0442x over previous
import numpy as np
import ml_dtypes
from contextlib import ExitStack

import concourse.bass as bass
import concourse.mybir as mybir
from concourse.bass_utils import run_bass_kernel_spmd

F32 = mybir.dt.float32
BF16 = mybir.dt.bfloat16
U32 = mybir.dt.uint32
ALU = mybir.AluOpType
AF = mybir.ActivationFunctionType
AX = mybir.AxisListType

CORES = 8
TOK = 1024
D = 512
H = 4096
DFF = 2048
FI = 11008
NL = 4
NT_G = 86
W_CHUNKS = [11, 11, 11, 11, 11, 11, 11, 9]
CH_START = [sum(W_CHUNKS[:i]) for i in range(8)]
EPS = 1e-5
CSZ = 32
JCH = TOK // CSZ
G_SCHED = [list(range(CH_START[l], CH_START[l] + W_CHUNKS[l])) for l in range(4)]
G_POST = list(range(CH_START[4], 86))

DEBUG = False


class Sem:
    def __init__(self, h):
        self.h = h
        self.n = 0


class Marks:
    def __init__(self, prev=None):
        self.d = {}
        self.prev = prev

    def mark(self, name, sem):
        self.d[name] = sem.n

    def get(self, name):
        return 0 if self.prev is None else self.prev.d[name]


def build(marks_prev=None):
    M = Marks(marks_prev)
    nc = bass.Bass("TRN2", target_bir_lowering=False)

    def din(name, shape, dt=F32):
        return nc.dram_tensor(name, shape, dt, kind="ExternalInput").ap()

    x_in = din("x_tok", [TOK, H])
    winT = din("winT", [H, D])
    w1T = din("w1T", [NL, D, DFF])
    w2T = din("w2T", [NL, DFF, D])
    woutT = din("woutT", [D, H])
    bin_p = din("bin_p", [128, 4])
    b1_p = din("b1_p", [128, NL * 16])
    b2_p = din("b2_p", [128, NL * 4])
    bout_p = din("bout_p", [128, 32])
    ln1w_p = din("ln1w_p", [128, NL * 4])
    ln1b_p = din("ln1b_p", [128, NL * 4])
    ln2w_p = din("ln2w_p", [128, NL * 4])
    ln2b_p = din("ln2b_p", [128, NL * 4])
    scl_p = din("scl_p", [128, NL * 4])
    pr_p = din("pr_p", [128, NL * 4])
    pi_p = din("pi_p", [128, NL * 4])
    lcr_p = din("lcr_p", [128, NL * 4])
    lci_p = din("lci_p", [128, NL * 4])
    lnlw_p = din("lnlw_p", [128, 4])
    lnlb_p = din("lnlb_p", [128, 4])
    qwgu = din("qwgu", [64, 2 * FI], U32)
    scgu = din("scgu", [4, 2 * FI])
    qzgu = din("qzgu", [4, 2 * FI // 8], U32)
    qwd = din("qwd", [1376, 512], U32)
    scd = din("scd", [86, 512])
    qzd = din("qzd", [86, 64], U32)
    idb = din("idb", [128, 128], BF16)
    idf = din("idf", [128, 128], F32)
    ones512 = din("ones512", [128, 128])
    sel = din("sel", [1, 8])
    ceps = din("ceps", [128, 1])
    cmh = din("cmh", [128, 1])

    out_d = nc.dram_tensor("out", [TOK, H], F32, kind="ExternalOutput").ap()

    dbg = "ExternalOutput" if DEBUG else "Internal"
    aggu_in = [nc.dram_tensor(f"aggu_in{i}", [512, 2 * W_CHUNKS[i] * 128], BF16)
               for i in range(8)]
    aggu_out = [nc.dram_tensor(f"aggu_out{i}", [CORES * 512, 2 * W_CHUNKS[i] * 128],
                               BF16, kind=dbg,
                               addr_space=("Local" if DEBUG else "Shared")) for i in range(8)]
    agd_in = nc.dram_tensor("agd_in", [FI, 512], BF16)
    agd_out = nc.dram_tensor("agd_out", [CORES * FI, 512], BF16, kind=dbg,
                             addr_space=("Local" if DEBUG else "Shared"))
    h_dram = nc.dram_tensor("h_dram", [NT_G, 128, TOK], BF16, kind=dbg)
    at_dram = nc.dram_tensor("at_dram", [32, 128, TOK], BF16, kind=dbg)
    dqsc_dram = nc.dram_tensor("dqsc_dram", [4, 1408], BF16)
    dqszq_dram = nc.dram_tensor("dqszq_dram", [4, 1408], BF16)
    ddsc_dram = nc.dram_tensor("ddsc_dram", [86, 512], BF16)
    ddszq_dram = nc.dram_tensor("ddszq_dram", [86, 512], BF16)
    agf_in = [nc.dram_tensor(f"agf_in{l}", [1, 1024], F32) for l in range(NL)]
    agf_out = [nc.dram_tensor(f"agf_out{l}", [CORES, 1024], F32) for l in range(NL)]

    with ExitStack() as ctx:
        def sbuf(name, shape, dt):
            return ctx.enter_context(nc.sbuf_tensor(name, shape, dt))

        def psum(name, shape, dt):
            return ctx.enter_context(nc.psum_tensor(name, shape, dt))

        def sem(name):
            return Sem(ctx.enter_context(nc.semaphore(name)))

        arenaA = sbuf("arenaA", [128, 32 * TOK], BF16)
        arenaG = sbuf("arenaG", [128, 24576], BF16)
        bufC = sbuf("bufC", [128, H], F32)
        bufD = sbuf("bufD", [128, H], F32)
        bufE = sbuf("bufE", [128, H], F32)
        bufF = sbuf("bufF", [128, 2048], F32)
        slabs = sbuf("slabs", [128, 12288], BF16)
        htst = sbuf("htst", [128, 2048], BF16)
        p_bin = sbuf("p_bin", [128, 4], F32)
        p_b1 = sbuf("p_b1", [128, NL * 16], F32)
        p_b2 = sbuf("p_b2", [128, NL * 4], F32)
        p_bout = sbuf("p_bout", [128, 32], F32)
        p_l1w = sbuf("p_l1w", [128, NL * 4], F32)
        p_l1b = sbuf("p_l1b", [128, NL * 4], F32)
        p_l2w = sbuf("p_l2w", [128, NL * 4], F32)
        p_l2b = sbuf("p_l2b", [128, NL * 4], F32)
        p_scl = sbuf("p_scl", [128, NL * 4], F32)
        p_pr = sbuf("p_pr", [128, NL * 4], F32)
        p_pi = sbuf("p_pi", [128, NL * 4], F32)
        p_lcr = sbuf("p_lcr", [128, NL * 4], F32)
        p_lci = sbuf("p_lci", [128, NL * 4], F32)
        p_lnlw = sbuf("p_lnlw", [128, 4], F32)
        p_lnlb = sbuf("p_lnlb", [128, 4], F32)
        idb_sb = sbuf("idb_sb", [128, 128], BF16)
        idf_sb = sbuf("idf_sb", [128, 128], F32)
        ones_sb = sbuf("ones_sb", [128, 128], F32)
        selb_sb = sbuf("selb_sb", [128, 8], F32)
        ab_sb = sbuf("ab_sb", [128, 16], F32)
        sqa_sb = sbuf("sqa_sb", [128, 44], F32)
        sqb_sb = sbuf("sqb_sb", [128, 44], F32)
        P32a = sbuf("P32a", [128, 128], F32)
        P32b = sbuf("P32b", [128, 128], F32)
        Qpa = sbuf("Qpa", [128, 128], F32)
        Qpb = sbuf("Qpb", [128, 128], F32)
        Sua = sbuf("Sua", [128, 128], F32)
        Sva = sbuf("Sva", [128, 128], F32)
        Sub = sbuf("Sub", [128, 128], F32)
        Svb = sbuf("Svb", [128, 128], F32)
        tmpa = sbuf("tmpa", [128, 128], F32)
        tmpb = sbuf("tmpb", [128, 128], F32)
        tmpc = sbuf("tmpc", [128, 128], F32)
        tmpd = sbuf("tmpd", [128, 128], F32)
        fco = sbuf("fco", [128, 16], F32)
        sall = sbuf("sall", [128, 64], F32)
        stat1 = sbuf("stat1", [128, 1], F32)
        stat2 = sbuf("stat2", [128, 1], F32)
        stat3 = sbuf("stat3", [128, 1], F32)
        ceps_sb = sbuf("ceps_sb", [128, 1], F32)
        cmh_sb = sbuf("cmh_sb", [128, 1], F32)

        ps = [psum(f"ps{k}", [128, 512], F32) for k in range(8)]

        def psb(k):
            return ps[k].ap().bitcast(BF16)[:, 0:128]

        s_pe = sem("s_pe")
        s_dve = sem("s_dve")
        s_act = sem("s_act")
        s_aq = sem("s_aq")
        s_sq = sem("s_sq")
        s_cc = sem("s_cc")
        s_pl = sem("s_pl")
        s_vs = sem("s_vs")
        s_ss = sem("s_ss")
        s_x = [sem("s_x0"), sem("s_x1")]
        s_ox = sem("s_ox")
        s_os = [sem("s_os0"), sem("s_os1")]
        s_gs = [sem(f"s_gs{k}") for k in range(3)]
        s_ds = [sem(f"s_ds{k}") for k in range(4)]
        s_qo = sem("s_qo")
        s_do = sem("s_do")
        s_dl = sem("s_dl")
        s_hw = [sem("s_hw0"), sem("s_hw1")]
        s_aw = [sem("s_aw0"), sem("s_aw1")]
        s_ar = [sem("s_ar0"), sem("s_ar1")]
        s_fw = sem("s_fw")
        s_fg = sem("s_fg")

        def carve(base, byte_off, dt, shape):
            esz = {F32: 4, U32: 4, BF16: 2}[dt]
            assert byte_off % 4 == 0
            b = base.ap().bitcast(dt)
            return b[: shape[0], byte_off // esz: byte_off // esz + shape[1]]

        dq_qwc = lambda: carve(arenaA, 0, U32, [64, 1408])
        dq_pp = lambda: carve(arenaA, 5632, U32, [128, 1408])
        dq_ppb = [lambda m=m: carve(arenaA, 11264 + m * 2816, BF16, [128, 1408]) for m in range(4)]
        dq_scb = lambda: carve(arenaA, 22528, BF16, [128, 1408])
        dq_szqb = lambda: carve(arenaA, 25344, BF16, [128, 1408])
        dq_scc = lambda: carve(arenaA, 28160, F32, [4, 1408])
        dq_qzc = lambda: carve(arenaA, 33792, U32, [4, 176])
        dq_zqc = lambda: carve(arenaA, 34496, U32, [4, 1408])
        dq_zqf = lambda: carve(arenaA, 40128, F32, [4, 1408])
        dq_szqs = lambda: carve(arenaA, 45760, BF16, [4, 1408])
        dq_scs = lambda: carve(arenaA, 48576, BF16, [4, 1408])
        dd_qwc = lambda: carve(arenaG, 0, U32, [128, 512])
        dd_pd = lambda: carve(arenaG, 2048, U32, [128, 512])
        dd_pdb = [lambda k=k: carve(arenaG, 4096 + k * 1024, BF16, [128, 512]) for k in range(8)]
        dd_scb = lambda: carve(arenaG, 12288, BF16, [128, 512])
        dd_szqb = lambda: carve(arenaG, 13312, BF16, [128, 512])
        dd_scd = lambda: carve(arenaG, 14336, F32, [86, 512])
        dd_qzd = lambda: carve(arenaG, 16384, U32, [86, 64])
        dd_zqd = lambda: carve(arenaG, 16640, U32, [86, 512])
        dd_zqdf = lambda: carve(arenaG, 18688, F32, [86, 512])
        dd_szqd = lambda: carve(arenaG, 20736, BF16, [86, 512])
        dd_scdb = lambda: carve(arenaG, 21760, BF16, [86, 512])
        winb = lambda: carve(arenaG, 0, BF16, [128, 16384])
        w1b = lambda: carve(arenaG, 0, BF16, [128, 8192])
        w2b = lambda: carve(arenaG, 16384, BF16, [128, 8192])
        y2b = lambda: carve(arenaG, 32768, BF16, [128, 4096])
        woutb = lambda: carve(arenaG, 0, BF16, [128, 16384])
        stg = [lambda k=k: carve(slabs, 0, F32, [128, 2048]) for k in range(2)]
        h1b = lambda: bufD.ap().bitcast(BF16)[:, 0:8192]
        gslab = [lambda k=k: carve(slabs, 8192 + k * 8192, BF16, [128, 4096]) for k in range(2)]
        wdsl = [lambda k=k: carve(slabs, k * 1024, BF16, [128, 512]) for k in range(4)]
        hsl = [lambda k=k: carve(slabs, 4096 + k * 2048, BF16, [128, 1024]) for k in range(4)]
        atsl = [lambda k=k: carve(htst, k * 2048, BF16, [128, 1024]) for k in range(2)]
        htsl = [lambda k=k: carve(htst, k * 2048, BF16, [128, 1024]) for k in range(2)]
        xnb = lambda: carve(bufE, 0, BF16, [128, H])
        gs = lambda pi_, hh: bufF.ap()[:, (pi_ % 2) * 1024 + hh * 512:(pi_ % 2) * 1024 + hh * 512 + 512]
        Sst2_u = sbuf("Sst2_u", [128, 264], F32)
        Sst2_v = sbuf("Sst2_v", [128, 264], F32)

        xnT = arenaA
        brsum = arenaA
        agf_sb = carve(arenaG, 40960, F32, [8, 1024])

        def pc(t, l, dp):
            return t.ap()[:, l * 4 + dp: l * 4 + dp + 1]

        cc_order = [("gu", 0), ("gu", 1), ("gu", 2), ("gu", 3), ("scan", 0),
                    ("gu", 4), ("scan", 1), ("gu", 5), ("scan", 2),
                    ("gu", 6), ("scan", 3), ("gu", 7), ("down",)]
        cc_idx = {e: k + 1 for k, e in enumerate(cc_order)}

        jobs = []
        for i in range(8):
            w = W_CHUNKS[i]
            jobs.append((i, 0, CH_START[i] * 128, w * 128))
            jobs.append((i, 1, FI + CH_START[i] * 128, w * 128))

        slab_seq = []
        for i in range(8):
            for t in range(W_CHUNKS[i]):
                slab_seq.append((i, t, 0))
                slab_seq.append((i, t, 1))

        with nc.Block() as block:
            @block.sync
            def _(e):
                def emit_slabs(ks):
                    for k in ks:
                        i, t, side = slab_seq[k]
                        if k >= 2:
                            e.wait_ge(s_pe.h, M.get(f"gu_mm_{k-2}"))
                        e.wait_ge(s_cc.h, cc_idx[("gu", i)])
                        w = W_CHUNKS[i]
                        coff = (t + side * w) * 128
                        src = aggu_out[i].ap().rearrange("(a p) c -> p a c", p=128)[:, :, coff:coff + 128]
                        sg = s_gs[k % 2]
                        e.dma_start(out=gslab[k % 2]().rearrange("p (a c) -> p a c", c=128),
                                    in_=src).then_inc(sg.h, 16)
                        sg.n += 16
                        M.mark(f"gu_slab_{k}", sg)

                smalls = [
                    (bin_p, p_bin), (b1_p, p_b1), (b2_p, p_b2), (bout_p, p_bout),
                    (ln1w_p, p_l1w), (ln1b_p, p_l1b), (ln2w_p, p_l2w), (ln2b_p, p_l2b),
                    (scl_p, p_scl), (pr_p, p_pr), (pi_p, p_pi), (lcr_p, p_lcr),
                    (lci_p, p_lci), (lnlw_p, p_lnlw), (lnlb_p, p_lnlb),
                    (idb, idb_sb), (idf, idf_sb), (ones512, ones_sb),
                    (ceps, ceps_sb), (cmh, cmh_sb),
                ]
                for src, dst in smalls:
                    e.dma_start(out=dst[:], in_=src).then_inc(s_sq.h, 16)
                    s_sq.n += 16
                e.dma_start(out=selb_sb[:], in_=sel.to_broadcast((128, 8))).then_inc(s_sq.h, 16)
                s_sq.n += 16
                M.mark("smalls", s_sq)

                for tt in range(8):
                    if tt >= 2:
                        e.wait_ge(s_dve.h, M.get(f"xn_done_{tt-2}"))
                    dst = bufC if tt % 2 == 0 else bufD
                    sx = s_x[tt % 2]
                    e.dma_start(out=dst[:], in_=x_in[tt * 128:(tt + 1) * 128, :]).then_inc(sx.h, 16)
                    sx.n += 16
                    M.mark(f"xload_{tt}", sx)

                for wc in range(8):
                    if wc >= 1:
                        e.wait_ge(s_dve.h, M.get(f"wincast_{wc-1}"))
                    e.dma_start(out=stg[wc % 2]().rearrange("p (a d) -> p a d", d=512),
                                in_=winT.rearrange("(a p) d -> p a d", p=128)
                                [:, wc * 4:(wc + 1) * 4, :]).then_inc(s_sq.h, 16)
                    s_sq.n += 16
                    M.mark(f"winload_{wc}", s_sq)

                for l in range(NL):
                    emit_slabs(range(2 * CH_START[l], 2 * (CH_START[l] + W_CHUNKS[l])))
                    for wc in range(4):
                        e.wait_ge(s_dve.h, M.get(f"w1cast_{l}_{wc}_free"))
                        e.dma_start(out=stg[wc % 2]().rearrange("p (a f) -> p a f", f=DFF),
                                    in_=w1T[l].rearrange("(a p) f -> p a f", p=128)
                                    [:, wc:wc + 1, :]).then_inc(s_sq.h, 16)
                        s_sq.n += 16
                        M.mark(f"w1load_{l}_{wc}", s_sq)
                    for wc in range(4):
                        e.wait_ge(s_dve.h, M.get(f"w2cast_{l}_{wc}_free"))
                        e.dma_start(out=stg[wc % 2]().rearrange("p (a d) -> p a d", d=512),
                                    in_=w2T[l].rearrange("(a p) d -> p a d", p=128)
                                    [:, wc * 4:(wc + 1) * 4, :]).then_inc(s_sq.h, 16)
                        s_sq.n += 16
                        M.mark(f"w2load_{l}_{wc}", s_sq)

                for wc in range(8):
                    e.wait_ge(s_dve.h, M.get(f"woutcast_{wc}_free"))
                    e.dma_start(out=stg[wc % 2](),
                                in_=woutT.rearrange("(a p) v -> p a v", p=128)
                                [:, wc // 2, (wc % 2) * 2048:(wc % 2) * 2048 + 2048]).then_inc(s_sq.h, 16)
                    s_sq.n += 16
                    M.mark(f"woutload_{wc}", s_sq)

                emit_slabs(range(2 * CH_START[4], 172))

                for hg in range(8):
                    for ft in range(NT_G):
                        k = hg * NT_G + ft
                        if k == 0:
                            e.wait_ge(s_cc.h, cc_idx[("down",)])
                            e.wait_ge(s_hw[0].h, M.get("h_all_w0"))
                            e.wait_ge(s_hw[1].h, M.get("h_all_w1"))
                            e.wait_ge(s_dve.h, M.get("gating_done"))
                        if k >= 4:
                            e.wait_ge(s_pe.h, M.get(f"dn_mm_{k-4}"))
                        src_w = agd_out.ap()[hg * FI + ft * 128: hg * FI + ft * 128 + 128, :]
                        sd = s_ds[k % 4]
                        e.dma_start(out=wdsl[k % 4](), in_=src_w).then_inc(sd.h, 16)
                        e.dma_start(out=hsl[k % 4](), in_=h_dram.ap()[ft]).then_inc(sd.h, 16)
                        sd.n += 32
                        M.mark(f"dn_slab_{k}", sd)

                for tt in range(8):
                    if tt == 0:
                        e.wait_ge(s_dve.h, M.get("dn_evac_done"))
                    if tt >= 1:
                        e.wait_ge(s_dve.h, M.get(f"oadd_{tt-1}"))
                    e.dma_start(out=bufC[:], in_=x_in[tt * 128:(tt + 1) * 128, :]).then_inc(s_ox.h, 16)
                    s_ox.n += 16
                    M.mark(f"oxload_{tt}", s_ox)
                    e.wait_ge(s_dve.h, M.get(f"oadd_{tt}"))
                    dstbuf = bufD if tt % 2 == 0 else bufE
                    so = s_os[tt % 2]
                    e.dma_start(out=out_d[tt * 128:(tt + 1) * 128, :],
                                in_=dstbuf[:]).then_inc(so.h, 16)
                    so.n += 16
                    M.mark(f"ostore_{tt}", so)
                e.wait_ge(s_os[0].h, M.get("ostore_6"))
                e.wait_ge(s_os[1].h, M.get("ostore_7"))

            @block.scalar
            def _(e):
                ssn = [0]

                def ss(ins):
                    ins.then_inc(s_ss.h, 1)
                    ssn[0] += 1
                    e.wait_ge(s_ss.h, ssn[0])

                def _gsilu1(pi_):
                    for hh in range(2):
                        e.wait_ge(s_pe.h, M.get(f"gmm_{pi_}_{hh}"))
                        if pi_ >= 2 and hh == 0:
                            e.wait_ge(s_dve.h, M.get(f"gating_{pi_-2}"))
                        e.activation(out=gs(pi_, hh), in_=ps[4 + hh][:],
                                     func=AF.Silu).then_inc(s_act.h, 1)
                        s_act.n += 1
                        M.mark(f"gsilu_{pi_}_{hh}", s_act)

                def _hwr1(pi_):
                    e.wait_ge(s_dve.h, M.get(f"gating_{pi_}"))
                    if pi_ >= 2:
                        e.wait_ge(s_hw[pi_ % 2].h, M.get(f"hwr_{pi_-2}"))
                    sh = s_hw[pi_ % 2]
                    e.dma_start(out=h_dram.ap()[pi_], in_=htsl[pi_ % 2]()).then_inc(sh.h, 16)
                    sh.n += 16
                    M.mark(f"hwr_{pi_}", sh)

                def emit_g_scalar(pis):
                    pis = list(pis)
                    for j, pi_ in enumerate(pis):
                        _gsilu1(pi_)
                        if j >= 2:
                            _hwr1(pis[j - 2])
                    for pi_ in pis[-2:] if len(pis) >= 2 else pis:
                        _hwr1(pi_)

                e.wait_ge(s_sq.h, M.get("smalls"))
                for jn, (i, side, qcol, wcols) in enumerate(jobs):
                    e.dma_start(out=dq_qwc()[:, :wcols], in_=qwgu[:, qcol:qcol + wcols]).then_inc(s_aq.h, 16)
                    e.dma_start(out=dq_scc()[:, :wcols], in_=scgu[:, qcol:qcol + wcols]).then_inc(s_aq.h, 16)
                    e.dma_start(out=dq_qzc()[:, :wcols // 8],
                                in_=qzgu[:, qcol // 8:(qcol + wcols) // 8]).then_inc(s_aq.h, 16)
                    s_aq.n += 48
                    M.mark(f"dq_ld_{jn}", s_aq)
                    e.wait_ge(s_dve.h, M.get(f"dq_src_{jn}"))
                    e.dma_start(out=dqsc_dram.ap()[:, :wcols], in_=dq_scs()[:, :wcols]).then_inc(s_aq.h, 16)
                    e.dma_start(out=dqszq_dram.ap()[:, :wcols], in_=dq_szqs()[:, :wcols]).then_inc(s_aq.h, 16)
                    s_aq.n += 32
                    e.wait_ge(s_aq.h, s_aq.n)
                    for g in range(4):
                        for hh in range(2):
                            p0 = hh * 64 + g * 16
                            e.dma_start(out=dq_scb()[p0:p0 + 16, :wcols],
                                        in_=dqsc_dram.ap()[g:g + 1, :wcols].to_broadcast((16, wcols))).then_inc(s_aq.h, 16)
                            e.dma_start(out=dq_szqb()[p0:p0 + 16, :wcols],
                                        in_=dqszq_dram.ap()[g:g + 1, :wcols].to_broadcast((16, wcols))).then_inc(s_aq.h, 16)
                            s_aq.n += 32
                    M.mark(f"dq_bc_{jn}", s_aq)
                    e.wait_ge(s_dve.h, M.get(f"dq_scaled_{jn}"))
                    w = W_CHUNKS[i]
                    for m in range(4):
                        for hh in range(2):
                            kk = 2 * m + hh
                            dst = aggu_in[i].ap()[kk::8, side * w * 128: side * w * 128 + wcols]
                            e.dma_start(out=dst, in_=dq_ppb[m]()[hh * 64:hh * 64 + 64, :wcols]).then_inc(s_qo.h, 16)
                            s_qo.n += 16
                    M.mark(f"dq_out_{jn}", s_qo)
                e.dma_start(out=dd_scd(), in_=scd).then_inc(s_aq.h, 16)
                e.dma_start(out=dd_qzd(), in_=qzd).then_inc(s_aq.h, 16)
                s_aq.n += 32
                M.mark("dd_ld", s_aq)
                e.wait_ge(s_dve.h, M.get("dd_src"))
                e.dma_start(out=ddsc_dram.ap(), in_=dd_scdb()).then_inc(s_aq.h, 16)
                e.dma_start(out=ddszq_dram.ap(), in_=dd_szqd()).then_inc(s_aq.h, 16)
                s_aq.n += 32
                e.wait_ge(s_aq.h, s_aq.n)
                for pt in range(11):
                    rp = min(128, 1376 - pt * 128)
                    e.wait_ge(s_dve.h, M.get(f"dd_free_{pt}"))
                    e.dma_start(out=dd_qwc()[:rp, :],
                                in_=qwd[pt * 128:pt * 128 + rp, :]).then_inc(s_dl.h, 16)
                    s_dl.n += 16
                    M.mark(f"dd_qw_{pt}", s_dl)
                    ng = (rp + 15) // 16
                    for gg in range(ng):
                        g = 8 * pt + gg
                        npt = min(16, rp - gg * 16)
                        e.dma_start(out=dd_scb()[gg * 16:gg * 16 + npt, :],
                                    in_=ddsc_dram.ap()[g:g + 1, :].to_broadcast((npt, 512))).then_inc(s_aq.h, 16)
                        e.dma_start(out=dd_szqb()[gg * 16:gg * 16 + npt, :],
                                    in_=ddszq_dram.ap()[g:g + 1, :].to_broadcast((npt, 512))).then_inc(s_aq.h, 16)
                        s_aq.n += 32
                    M.mark(f"dd_bc_{pt}", s_aq)
                    e.wait_ge(s_dve.h, M.get(f"dd_scaled_{pt}"))
                    for kk in range(8):
                        s0 = 8 * 128 * pt + kk
                        dst = agd_in.ap()[s0: s0 + 8 * (rp - 1) + 1: 8, :]
                        e.dma_start(out=dst, in_=dd_pdb[kk]()[:rp, :]).then_inc(s_do.h, 16)
                        s_do.n += 16
                    M.mark(f"dd_out_{pt}", s_do)

                for tt in range(8):
                    src = bufC if tt % 2 == 0 else bufD
                    if tt == 0:
                        e.wait_ge(s_qo.h, M.get("dq_out_15"))
                    if tt >= 1:
                        e.wait_ge(s_pe.h, M.get(f"xtr_{tt-1}_31"))
                    e.wait_ge(s_x[tt % 2].h, M.get(f"xload_{tt}"))
                    e.wait_ge(s_dve.h, M.get(f"xstat_z_{tt}"))
                    e.activation(out=bufE[:], in_=src[:], func=AF.Square,
                                 accum_out=stat2[:]).then_inc(s_act.h, 1)
                    s_act.n += 1
                    M.mark(f"xsq_{tt}", s_act)
                    e.wait_ge(s_dve.h, M.get(f"xvar_{tt}"))
                    ss(e.activation(out=stat3[:], in_=stat3[:], func=AF.Ln, bias=ceps_sb[:]))
                    e.activation(out=stat3[:], in_=stat3[:], func=AF.Exp, scale=cmh_sb[:]).then_inc(s_act.h, 1)
                    s_act.n += 1
                    M.mark(f"xrstd_{tt}", s_act)
                    for q in range(32):
                        e.wait_ge(s_pe.h, M.get(f"xtr_{tt}_{q}"))
                        e.copy(out=xnT.ap()[:, q * 1024 + tt * 128: q * 1024 + tt * 128 + 128],
                               in_=psb(q % 2)).then_inc(s_act.h, 1)
                        s_act.n += 1
                        M.mark(f"xev_{tt}_{q}", s_act)
                M.mark("xn_all", s_act)

                for dp in range(4):
                    for hh in range(2):
                        k = dp * 2 + hh
                        e.wait_ge(s_pe.h, M.get(f"emb_mm_{k}"))
                        e.activation(out=bufC[:, dp * 1024 + hh * 512: dp * 1024 + hh * 512 + 512],
                                     in_=ps[k % 2][:], func=AF.Identity,
                                     bias=p_bin.ap()[:, dp:dp + 1]).then_inc(s_act.h, 1)
                        s_act.n += 1
                        M.mark(f"emb_ev_{k}", s_act)

                def ln_scalar(tag):
                    for dp in range(4):
                        e.wait_ge(s_dve.h, M.get(f"{tag}_sqrdy"))
                        e.activation(out=bufE[:, dp * 1024:(dp + 1) * 1024],
                                     in_=bufC[:, dp * 1024:(dp + 1) * 1024],
                                     func=AF.Square).then_inc(s_act.h, 1)
                        s_act.n += 1
                        M.mark(f"{tag}_sq_{dp}", s_act)
                    e.wait_ge(s_dve.h, M.get(f"{tag}_var"))
                    rv = bufF.ap()[:, 1024:2048]
                    ss(e.activation(out=rv, in_=rv, func=AF.Ln, bias=ceps_sb[:]))
                    e.activation(out=rv, in_=rv, func=AF.Exp, scale=cmh_sb[:]).then_inc(s_act.h, 1)
                    s_act.n += 1
                    M.mark(f"{tag}_rstd", s_act)

                for l in range(NL):
                    e.wait_ge(s_dve.h, M.get(f"p_r2_{l}"))
                    ss(e.activation(out=ab_sb[:, 8:12], in_=ab_sb[:, 8:12], func=AF.Sqrt))
                    e.activation(out=ab_sb[:, 12:16], in_=ab_sb[:, 8:12], func=AF.Tanh).then_inc(s_act.h, 1)
                    s_act.n += 1
                    M.mark(f"p_act_{l}", s_act)
                    ln_scalar(f"l1_{l}")
                    for dp in range(4):
                        e.wait_ge(s_dve.h, M.get(f"l1_{l}_app_{dp}"))
                        e.activation(out=bufD[:, dp * 1024:(dp + 1) * 1024],
                                     in_=bufD[:, dp * 1024:(dp + 1) * 1024],
                                     func=AF.Silu).then_inc(s_act.h, 1)
                        s_act.n += 1
                        M.mark(f"l1_{l}_silu_{dp}", s_act)
                    emit_g_scalar(G_SCHED[l])
                    ln_scalar(f"l2_{l}")
                    for hh in range(2):
                        for m in range(16):
                            k = hh * 16 + m
                            e.wait_ge(s_pe.h, M.get(f"w1mm_{l}_{k}"))
                            e.activation(out=h1b()[:, m * 512:(m + 1) * 512],
                                         in_=ps[k % 2][:], func=AF.Silu,
                                         bias=p_b1.ap()[:, l * 16 + m: l * 16 + m + 1]).then_inc(s_act.h, 1)
                            s_act.n += 1
                            M.mark(f"w1ev_{l}_{k}", s_act)
                ln_scalar("lnl")
                for vt in range(32):
                    for hh in range(2):
                        k = vt * 2 + hh
                        e.wait_ge(s_pe.h, M.get(f"wo_mm_{k}"))
                        e.activation(out=htsl[vt % 2]()[:, hh * 512:hh * 512 + 512],
                                     in_=ps[k % 4][:], func=AF.Identity,
                                     bias=p_bout.ap()[:, vt:vt + 1]).then_inc(s_act.h, 1)
                        s_act.n += 1
                    M.mark(f"wo_ev_{vt}", s_act)
                    if vt >= 2:
                        e.wait_ge(s_aw[vt % 2].h, M.get(f"at_wr_{vt-2}"))
                    sa = s_aw[vt % 2]
                    e.dma_start(out=at_dram.ap()[vt], in_=htsl[vt % 2]()).then_inc(sa.h, 16)
                    sa.n += 16
                    M.mark(f"at_wr_{vt}", sa)
                M.mark("attn_sc_done", s_act)

                emit_g_scalar(G_POST)
                M.mark("h_all_w0", s_hw[0])
                M.mark("h_all_w1", s_hw[1])

                for hg in range(8):
                    for q in range(4):
                        k = hg * 4 + q
                        if k >= 2:
                            e.wait_ge(s_dve.h, M.get(f"dnev_{k-2}"))
                        sr = s_ar[k % 2]
                        e.dma_start(out=atsl[k % 2](), in_=at_dram.ap()[hg * 4 + q]).then_inc(sr.h, 16)
                        sr.n += 16
                        M.mark(f"at_rd_{k}", sr)

            @block.vector
            def _(e):
                vsn = [0]

                def vs(ins):
                    ins.then_inc(s_vs.h, 1)
                    vsn[0] += 1
                    e.wait_ge(s_vs.h, vsn[0])

                def emit_gating(pis):
                    for pi_ in pis:
                        upb = [6, 7] if pi_ % 2 == 0 else [2, 3]
                        if pi_ >= 2:
                            e.wait_ge(s_hw[pi_ % 2].h, M.get(f"hwr_{pi_-2}"))
                        last = None
                        for hh in range(2):
                            e.wait_ge(s_act.h, M.get(f"gsilu_{pi_}_{hh}"))
                            e.wait_ge(s_pe.h, M.get(f"umm_{pi_}_{hh}"))
                            last = e.tensor_mul(htsl[pi_ % 2]()[:, hh * 512:hh * 512 + 512],
                                                gs(pi_, hh), ps[upb[hh]][:])
                        last.then_inc(s_dve.h, 1)
                        s_dve.n += 1
                        M.mark(f"gating_{pi_}", s_dve)

                for jn, (i, side, qcol, wcols) in enumerate(jobs):
                    e.wait_ge(s_aq.h, M.get(f"dq_ld_{jn}"))
                    last = None
                    for kk in range(8):
                        last = e.tensor_scalar(out=dq_zqc()[:, kk:wcols:8], in0=dq_qzc()[:, :wcols // 8],
                                               scalar1=float(4 * kk), scalar2=15.0,
                                               op0=ALU.logical_shift_right, op1=ALU.bitwise_and)
                    vs(last)
                    e.tensor_copy(out=dq_zqf()[:, :wcols], in_=dq_zqc()[:, :wcols])
                    e.scalar_tensor_tensor(out=dq_szqs()[:, :wcols], in0=dq_zqf()[:, :wcols],
                                           scalar=1.0, in1=dq_scc()[:, :wcols],
                                           op0=ALU.add, op1=ALU.mult)
                    e.tensor_copy(out=dq_scs()[:, :wcols], in_=dq_scc()[:, :wcols]).then_inc(s_dve.h, 1)
                    s_dve.n += 1
                    M.mark(f"dq_src_{jn}", s_dve)
                    for m in range(4):
                        for hh in range(2):
                            kk = 2 * m + hh
                            e.tensor_scalar(out=dq_pp()[hh * 64:hh * 64 + 64, :wcols],
                                            in0=dq_qwc()[:, :wcols],
                                            scalar1=float(4 * kk), scalar2=15.0,
                                            op0=ALU.logical_shift_right, op1=ALU.bitwise_and)
                        e.tensor_copy(out=dq_ppb[m]()[:, :wcols], in_=dq_pp()[:, :wcols])
                    e.wait_ge(s_aq.h, M.get(f"dq_bc_{jn}"))
                    last = None
                    for m in range(4):
                        e.tensor_mul(dq_ppb[m]()[:, :wcols], dq_ppb[m]()[:, :wcols], dq_scb()[:, :wcols])
                    for m in range(4):
                        last = e.tensor_sub(dq_ppb[m]()[:, :wcols], dq_ppb[m]()[:, :wcols], dq_szqb()[:, :wcols])
                    last.then_inc(s_dve.h, 1)
                    s_dve.n += 1
                    M.mark(f"dq_scaled_{jn}", s_dve)
                    e.wait_ge(s_qo.h, M.get(f"dq_out_{jn}"))
                e.wait_ge(s_aq.h, M.get("dd_ld"))
                last = None
                for kk in range(8):
                    last = e.tensor_scalar(out=dd_zqd()[:, kk::8], in0=dd_qzd(),
                                           scalar1=float(4 * kk), scalar2=15.0,
                                           op0=ALU.logical_shift_right, op1=ALU.bitwise_and)
                vs(last)
                e.tensor_copy(out=dd_zqdf(), in_=dd_zqd())
                e.scalar_tensor_tensor(out=dd_szqd(), in0=dd_zqdf(), scalar=1.0,
                                       in1=dd_scd(), op0=ALU.add, op1=ALU.mult)
                e.tensor_copy(out=dd_scdb(), in_=dd_scd()).then_inc(s_dve.h, 1)
                s_dve.n += 1
                M.mark("dd_src", s_dve)
                for pt in range(11):
                    rp = min(128, 1376 - pt * 128)
                    M.mark(f"dd_free_{pt}", s_dve)
                    e.wait_ge(s_dl.h, M.get(f"dd_qw_{pt}"))
                    e.wait_ge(s_aq.h, M.get(f"dd_bc_{pt}"))
                    last = None
                    for kk in range(8):
                        vs(e.tensor_scalar(out=dd_pd()[:rp, :], in0=dd_qwc()[:rp, :],
                                           scalar1=float(4 * kk), scalar2=15.0,
                                           op0=ALU.logical_shift_right, op1=ALU.bitwise_and))
                        vs(e.tensor_copy(out=dd_pdb[kk]()[:rp, :], in_=dd_pd()[:rp, :]))
                        vs(e.tensor_mul(dd_pdb[kk]()[:rp, :], dd_pdb[kk]()[:rp, :], dd_scb()[:rp, :]))
                        last = e.tensor_sub(dd_pdb[kk]()[:rp, :], dd_pdb[kk]()[:rp, :], dd_szqb()[:rp, :])
                    last.then_inc(s_dve.h, 1)
                    s_dve.n += 1
                    M.mark(f"dd_scaled_{pt}", s_dve)
                    e.wait_ge(s_do.h, M.get(f"dd_out_{pt}"))

                for tt in range(8):
                    src = bufC if tt % 2 == 0 else bufD
                    e.memset(stat2[:], 0.0).then_inc(s_dve.h, 1)
                    s_dve.n += 1
                    M.mark(f"xstat_z_{tt}", s_dve)
                    e.wait_ge(s_x[tt % 2].h, M.get(f"xload_{tt}"))
                    vs(e.tensor_reduce(out=stat1[:], in_=src[:], axis=AX.X, op=ALU.add))
                    vs(e.tensor_scalar(out=stat1[:], in0=stat1[:], scalar1=1.0 / H,
                                       scalar2=None, op0=ALU.mult))
                    e.wait_ge(s_act.h, M.get(f"xsq_{tt}"))
                    vs(e.tensor_mul(stat3[:], stat1[:], stat1[:]))
                    e.scalar_tensor_tensor(out=stat3[:], in0=stat2[:], scalar=1.0 / H,
                                           in1=stat3[:], op0=ALU.mult,
                                           op1=ALU.subtract).then_inc(s_dve.h, 1)
                    s_dve.n += 1
                    M.mark(f"xvar_{tt}", s_dve)
                    e.wait_ge(s_act.h, M.get(f"xrstd_{tt}"))
                    if tt >= 1:
                        e.wait_ge(s_pe.h, M.get(f"xtr_{tt-1}_31"))
                    e.tensor_scalar(out=xnb(), in0=src[:], scalar1=stat1[:],
                                    scalar2=stat3[:], op0=ALU.subtract,
                                    op1=ALU.mult).then_inc(s_dve.h, 1)
                    s_dve.n += 1
                    M.mark(f"xn_done_{tt}", s_dve)

                e.wait_ge(s_do.h, M.get("dd_out_10"))
                for wc in range(8):
                    e.wait_ge(s_sq.h, M.get(f"winload_{wc}"))
                    e.tensor_copy(out=winb()[:, wc * 2048:(wc + 1) * 2048],
                                  in_=stg[wc % 2]()).then_inc(s_dve.h, 1)
                    s_dve.n += 1
                    M.mark(f"wincast_{wc}", s_dve)
                M.mark("win_ready", s_dve)

                def ln_vector(tag, src, dst_ap, wt, bt):
                    M.mark(f"{tag}_sqrdy", s_dve)
                    e.wait_ge(s_pe.h, M.get(f"{tag}_mm"))
                    m_sb = bufF.ap()[:, 0:1024]
                    var = bufF.ap()[:, 1024:2048]
                    e.tensor_copy(out=m_sb[:, 0:512], in_=ps[0][:])
                    e.tensor_copy(out=m_sb[:, 512:1024], in_=ps[1][:])
                    e.tensor_mul(var[:, 0:512], m_sb[:, 0:512], m_sb[:, 0:512])
                    e.tensor_mul(var[:, 512:1024], m_sb[:, 512:1024], m_sb[:, 512:1024])
                    e.tensor_sub(var[:, 0:512], ps[2][:], var[:, 0:512])
                    e.tensor_sub(var[:, 512:1024], ps[3][:], var[:, 512:1024]).then_inc(s_dve.h, 1)
                    s_dve.n += 1
                    M.mark(f"{tag}_var", s_dve)
                    e.wait_ge(s_act.h, M.get(f"{tag}_rstd"))
                    for dp in range(4):
                        blk = slice(dp * 1024, (dp + 1) * 1024)
                        e.tensor_sub(bufE[:, blk], src[:, blk], m_sb)
                        e.tensor_mul(bufE[:, blk], bufE[:, blk], var)
                        e.tensor_scalar(out=dst_ap[:, blk], in0=bufE[:, blk],
                                        scalar1=wt(dp), scalar2=bt(dp),
                                        op0=ALU.mult, op1=ALU.add).then_inc(s_dve.h, 1)
                        s_dve.n += 1
                        M.mark(f"{tag}_app_{dp}", s_dve)

                def cmul_ss(ar_ap, ai_ap, br_ap, bi_ap, or_ap, oi_ap):
                    e.tensor_mul(tmpa[:, 0:1], ar_ap, br_ap)
                    e.tensor_mul(tmpb[:, 0:1], ai_ap, bi_ap)
                    e.tensor_mul(oi_ap, ar_ap, bi_ap)
                    e.tensor_mul(tmpb[:, 1:2], ai_ap, br_ap)
                    e.tensor_sub(or_ap, tmpa[:, 0:1], tmpb[:, 0:1])
                    e.tensor_add(oi_ap, oi_ap, tmpb[:, 1:2])

                apsd = lambda dp, k: sqa_sb.ap()[:, dp * 11 + k: dp * 11 + k + 1]
                bpsd = lambda dp, k: sqb_sb.ap()[:, dp * 11 + k: dp * 11 + k + 1]
                SL_u = lambda dp: Sst2_u.ap()[:, dp * 32:(dp + 1) * 32]
                SL_v = lambda dp: Sst2_v.ap()[:, dp * 32:(dp + 1) * 32]
                Fall_u = lambda dp: Sst2_u.ap()[:, 128 + dp * 8:128 + dp * 8 + 8]
                Fall_v = lambda dp: Sst2_v.ap()[:, 128 + dp * 8:128 + dp * 8 + 8]
                P_u = lambda dp: Sst2_u.ap()[:, 160 + dp * 26:160 + dp * 26 + 26]

                def build_pow_table(outa, outb, n_total, aps, bps, k0, init_one):
                    if init_one:
                        e.memset(outa[:, 0:1], 1.0)
                        e.memset(outb[:, 0:1], 0.0)
                    else:
                        e.tensor_copy(out=outa[:, 0:1], in_=aps(0))
                        e.tensor_copy(out=outb[:, 0:1], in_=bps(0))
                    n = 1
                    k = k0
                    while n < n_total:
                        mlen = min(n, n_total - n)
                        e.tensor_scalar_mul(tmpa[:, 0:mlen], outb[:, 0:mlen], bps(k))
                        e.scalar_tensor_tensor(out=outa[:, n:n + mlen], in0=outa[:, 0:mlen],
                                               scalar=aps(k), in1=tmpa[:, 0:mlen],
                                               op0=ALU.mult, op1=ALU.subtract)
                        e.tensor_scalar_mul(tmpb[:, 0:mlen], outa[:, 0:mlen], bps(k))
                        e.scalar_tensor_tensor(out=outb[:, n:n + mlen], in0=outb[:, 0:mlen],
                                               scalar=aps(k), in1=tmpb[:, 0:mlen],
                                               op0=ALU.mult, op1=ALU.add)
                        n += mlen
                        k += 1

                for l in range(NL):
                    prl = p_pr.ap()[:, l * 4:(l + 1) * 4]
                    pil = p_pi.ap()[:, l * 4:(l + 1) * 4]
                    e.tensor_mul(ab_sb[:, 8:12], prl, prl)
                    vs(e.tensor_mul(tmpb[:, 0:4], pil, pil))
                    vs(e.tensor_add(ab_sb[:, 8:12], ab_sb[:, 8:12], tmpb[:, 0:4]))
                    e.tensor_scalar(out=ab_sb[:, 8:12], in0=ab_sb[:, 8:12],
                                    scalar1=1e-30, scalar2=None, op0=ALU.max).then_inc(s_dve.h, 1)
                    s_dve.n += 1
                    M.mark(f"p_r2_{l}", s_dve)
                    e.wait_ge(s_act.h, M.get(f"p_act_{l}"))
                    vs(e.reciprocal(tmpa[:, 0:4], ab_sb[:, 8:12]))
                    vs(e.tensor_mul(tmpa[:, 0:4], tmpa[:, 0:4], ab_sb[:, 12:16]))
                    e.tensor_mul(ab_sb[:, 0:4], prl, tmpa[:, 0:4])
                    vs(e.tensor_mul(ab_sb[:, 4:8], pil, tmpa[:, 0:4]))
                    for dp in range(4):
                        e.tensor_copy(out=sqa_sb[:, dp * 11:dp * 11 + 1], in_=ab_sb[:, dp:dp + 1])
                    for dp in range(4):
                        e.tensor_copy(out=sqb_sb[:, dp * 11:dp * 11 + 1], in_=ab_sb[:, 4 + dp:4 + dp + 1])
                    for k in range(10):
                        for dp in range(4):
                            e.tensor_mul(tmpa[:, dp:dp + 1], apsd(dp, k), apsd(dp, k))
                        for dp in range(4):
                            e.tensor_mul(tmpb[:, dp:dp + 1], bpsd(dp, k), bpsd(dp, k))
                        for dp in range(4):
                            e.tensor_mul(tmpc[:, dp:dp + 1], apsd(dp, k), bpsd(dp, k))
                        for dp in range(4):
                            e.tensor_sub(sqa_sb[:, dp * 11 + k + 1:dp * 11 + k + 2],
                                         tmpa[:, dp:dp + 1], tmpb[:, dp:dp + 1])
                        for dp in range(4):
                            e.tensor_add(sqb_sb[:, dp * 11 + k + 1:dp * 11 + k + 2],
                                         tmpc[:, dp:dp + 1], tmpc[:, dp:dp + 1])
                    ln_vector(f"l1_{l}", bufC, bufD.ap(), lambda dp: pc(p_l1w, l, dp),
                              lambda dp: pc(p_l1b, l, dp))
                    for dp in range(4):
                        e.wait_ge(s_act.h, M.get(f"l1_{l}_silu_{dp}"))
                    e.wait_ge(s_pl.h, M.get(f"vmemset_{l}"))
                    u3 = [bufD.ap()[:, dp * 1024:dp * 1024 + 1024]
                          .rearrange("p (j k) -> p j k", k=CSZ) for dp in range(4)]
                    v3 = [bufE.ap()[:, dp * 1024:dp * 1024 + 1024]
                          .rearrange("p (j k) -> p j k", k=CSZ) for dp in range(4)]
                    LN_ = lambda t, dp, a0, b0: t.ap()[:, dp * 32 + a0: dp * 32 + b0]
                    pend_g = list(G_SCHED[l])
                    for kk in range(1, CSZ):
                        if kk % 5 == 0 and pend_g:
                            emit_gating([pend_g.pop(0)])
                        for dp in range(4):
                            e.tensor_scalar_mul(LN_(tmpa, dp, 0, JCH), v3[dp][:, :, kk - 1], bpsd(dp, 0))
                        for dp in range(4):
                            e.scalar_tensor_tensor(out=LN_(tmpb, dp, 0, JCH), in0=u3[dp][:, :, kk - 1],
                                                   scalar=apsd(dp, 0), in1=LN_(tmpa, dp, 0, JCH),
                                                   op0=ALU.mult, op1=ALU.subtract)
                        for dp in range(4):
                            e.tensor_add(u3[dp][:, :, kk], u3[dp][:, :, kk], LN_(tmpb, dp, 0, JCH))
                        for dp in range(4):
                            e.tensor_scalar_mul(LN_(tmpc, dp, 0, JCH), u3[dp][:, :, kk - 1], bpsd(dp, 0))
                        for dp in range(4):
                            e.scalar_tensor_tensor(out=v3[dp][:, :, kk], in0=v3[dp][:, :, kk - 1],
                                                   scalar=apsd(dp, 0), in1=LN_(tmpc, dp, 0, JCH),
                                                   op0=ALU.mult, op1=ALU.add)
                    for dp in range(4):
                        e.memset(LN_(Sua, dp, 0, 1), 0.0)
                    for dp in range(4):
                        e.memset(LN_(Sva, dp, 0, 1), 0.0)
                    for dp in range(4):
                        e.tensor_copy(out=LN_(Sua, dp, 1, JCH), in_=u3[dp][:, 0:JCH - 1, CSZ - 1])
                    for dp in range(4):
                        e.tensor_copy(out=LN_(Sva, dp, 1, JCH), in_=v3[dp][:, 0:JCH - 1, CSZ - 1])
                    cur, oth = (Sua, Sva), (Sub, Svb)
                    s = 1
                    for pas in range(5):
                        cu, cv = cur
                        ou, ov = oth
                        for dp in range(4):
                            e.tensor_copy(out=LN_(ou, dp, 0, s), in_=LN_(cu, dp, 0, s))
                        for dp in range(4):
                            e.tensor_copy(out=LN_(ov, dp, 0, s), in_=LN_(cv, dp, 0, s))
                        for dp in range(4):
                            e.tensor_scalar_mul(LN_(tmpa, dp, 0, JCH - s), LN_(cv, dp, 0, JCH - s), bpsd(dp, 5 + pas))
                        for dp in range(4):
                            e.scalar_tensor_tensor(out=LN_(tmpb, dp, 0, JCH - s), in0=LN_(cu, dp, 0, JCH - s),
                                                   scalar=apsd(dp, 5 + pas), in1=LN_(tmpa, dp, 0, JCH - s),
                                                   op0=ALU.mult, op1=ALU.subtract)
                        for dp in range(4):
                            e.tensor_add(LN_(ou, dp, s, JCH), LN_(cu, dp, s, JCH), LN_(tmpb, dp, 0, JCH - s))
                        for dp in range(4):
                            e.tensor_scalar_mul(LN_(tmpc, dp, 0, JCH - s), LN_(cu, dp, 0, JCH - s), bpsd(dp, 5 + pas))
                        for dp in range(4):
                            e.scalar_tensor_tensor(out=LN_(tmpd, dp, 0, JCH - s), in0=LN_(cv, dp, 0, JCH - s),
                                                   scalar=apsd(dp, 5 + pas), in1=LN_(tmpc, dp, 0, JCH - s),
                                                   op0=ALU.mult, op1=ALU.add)
                        for dp in range(4):
                            e.tensor_add(LN_(ov, dp, s, JCH), LN_(cv, dp, s, JCH), LN_(tmpd, dp, 0, JCH - s))
                        cur, oth = oth, cur
                        s *= 2
                    Su, Sv = cur
                    for dp in range(4):
                        e.tensor_copy(out=SL_u(dp), in_=LN_(Su, dp, 0, JCH))
                    for dp in range(4):
                        e.tensor_copy(out=SL_v(dp), in_=LN_(Sv, dp, 0, JCH))
                    for dp in range(4):
                        e.tensor_scalar_mul(tmpa[:, dp:dp + 1], LN_(Sv, dp, JCH - 1, JCH), bpsd(dp, 5))
                    for dp in range(4):
                        e.scalar_tensor_tensor(out=tmpb[:, dp:dp + 1], in0=LN_(Su, dp, JCH - 1, JCH),
                                               scalar=apsd(dp, 5), in1=tmpa[:, dp:dp + 1],
                                               op0=ALU.mult, op1=ALU.subtract)
                    for dp in range(4):
                        e.tensor_add(fco[:, 2 * dp:2 * dp + 1], tmpb[:, dp:dp + 1],
                                     u3[dp][:, JCH - 1, CSZ - 1:CSZ])
                    for dp in range(4):
                        e.tensor_scalar_mul(tmpc[:, dp:dp + 1], LN_(Su, dp, JCH - 1, JCH), bpsd(dp, 5))
                    for dp in range(4):
                        e.scalar_tensor_tensor(out=tmpd[:, dp:dp + 1], in0=LN_(Sv, dp, JCH - 1, JCH),
                                               scalar=apsd(dp, 5), in1=tmpc[:, dp:dp + 1],
                                               op0=ALU.mult, op1=ALU.add)
                    last = None
                    for dp in range(4):
                        last = e.tensor_add(fco[:, 2 * dp + 1:2 * dp + 2], tmpd[:, dp:dp + 1],
                                            v3[dp][:, JCH - 1, CSZ - 1:CSZ])
                    last.then_inc(s_dve.h, 1)
                    s_dve.n += 1
                    M.mark(f"F_ready_{l}", s_dve)
                    emit_gating(pend_g)
                    e.wait_ge(s_pe.h, M.get(f"Ftr_{l}"))
                    for dp in range(4):
                        e.tensor_copy(out=Fall_u(dp), in_=ps[0].ap()[:, dp * 8:dp * 8 + 8])
                    for dp in range(4):
                        e.tensor_copy(out=Fall_v(dp), in_=ps[1].ap()[:, dp * 8:dp * 8 + 8])
                    e.memset(tmpa[:, 0:1], 0.0).then_inc(s_dve.h, 1)
                    s_dve.n += 1
                    M.mark(f"Fcp_{l}", s_dve)
                    sall_u = lambda dp, k: sall.ap()[:, dp * 16 + k: dp * 16 + k + 1]
                    sall_v = lambda dp, k: sall.ap()[:, dp * 16 + 8 + k: dp * 16 + 8 + k + 1]
                    for dp in range(4):
                        e.tensor_copy(out=sall_u(dp, 0), in_=pc(p_lcr, l, dp))
                    for dp in range(4):
                        e.tensor_copy(out=sall_v(dp, 0), in_=pc(p_lci, l, dp))
                    for kk in range(7):
                        for dp in range(4):
                            e.tensor_scalar_mul(tmpa[:, dp:dp + 1], sall_v(dp, kk), bpsd(dp, 10))
                        for dp in range(4):
                            e.scalar_tensor_tensor(out=tmpb[:, dp:dp + 1], in0=sall_u(dp, kk),
                                                   scalar=apsd(dp, 10), in1=tmpa[:, dp:dp + 1],
                                                   op0=ALU.mult, op1=ALU.subtract)
                        for dp in range(4):
                            e.tensor_add(sall_u(dp, kk + 1), tmpb[:, dp:dp + 1],
                                         Fall_u(dp)[:, kk:kk + 1])
                        for dp in range(4):
                            e.tensor_scalar_mul(tmpc[:, dp:dp + 1], sall_u(dp, kk), bpsd(dp, 10))
                        for dp in range(4):
                            e.scalar_tensor_tensor(out=tmpd[:, dp:dp + 1], in0=sall_v(dp, kk),
                                                   scalar=apsd(dp, 10), in1=tmpc[:, dp:dp + 1],
                                                   op0=ALU.mult, op1=ALU.add)
                        for dp in range(4):
                            e.tensor_add(sall_v(dp, kk + 1), tmpd[:, dp:dp + 1],
                                         Fall_v(dp)[:, kk:kk + 1])
                    for dp in range(4):
                        e.tensor_mul(tmpa[:, dp * 8:dp * 8 + 8], sall.ap()[:, dp * 16:dp * 16 + 8], selb_sb[:])
                    for dp in range(4):
                        e.tensor_reduce(out=fco[:, 8 + 2 * dp:9 + 2 * dp], in_=tmpa[:, dp * 8:dp * 8 + 8],
                                        axis=AX.X, op=ALU.add)
                    for dp in range(4):
                        e.tensor_mul(tmpb[:, dp * 8:dp * 8 + 8], sall.ap()[:, dp * 16 + 8:dp * 16 + 16], selb_sb[:])
                    for dp in range(4):
                        e.tensor_reduce(out=fco[:, 9 + 2 * dp:10 + 2 * dp], in_=tmpb[:, dp * 8:dp * 8 + 8],
                                        axis=AX.X, op=ALU.add)
                    def pow4(outa, outb, n_total, kbase, init_one):
                        if init_one:
                            for dp in range(4):
                                e.memset(outa.ap()[:, dp * 32:dp * 32 + 1], 1.0)
                            for dp in range(4):
                                e.memset(outb.ap()[:, dp * 32:dp * 32 + 1], 0.0)
                        else:
                            for dp in range(4):
                                e.tensor_copy(out=outa.ap()[:, dp * 32:dp * 32 + 1], in_=apsd(dp, kbase))
                            for dp in range(4):
                                e.tensor_copy(out=outb.ap()[:, dp * 32:dp * 32 + 1], in_=bpsd(dp, kbase))
                        n, k = 1, kbase
                        while n < n_total:
                            mlen = min(n, n_total - n)
                            for dp in range(4):
                                e.tensor_scalar_mul(tmpa[:, dp * 32:dp * 32 + mlen],
                                                    outb.ap()[:, dp * 32:dp * 32 + mlen], bpsd(dp, k))
                            for dp in range(4):
                                e.scalar_tensor_tensor(out=outa.ap()[:, dp * 32 + n:dp * 32 + n + mlen],
                                                       in0=outa.ap()[:, dp * 32:dp * 32 + mlen],
                                                       scalar=apsd(dp, k), in1=tmpa[:, dp * 32:dp * 32 + mlen],
                                                       op0=ALU.mult, op1=ALU.subtract)
                            for dp in range(4):
                                e.tensor_scalar_mul(tmpb[:, dp * 32:dp * 32 + mlen],
                                                    outa.ap()[:, dp * 32:dp * 32 + mlen], bpsd(dp, k))
                            for dp in range(4):
                                e.scalar_tensor_tensor(out=outb.ap()[:, dp * 32 + n:dp * 32 + n + mlen],
                                                       in0=outb.ap()[:, dp * 32:dp * 32 + mlen],
                                                       scalar=apsd(dp, k), in1=tmpb[:, dp * 32:dp * 32 + mlen],
                                                       op0=ALU.mult, op1=ALU.add)
                            n += mlen
                            k += 1
                    pow4(P32a, P32b, CSZ, 0, init_one=False)
                    pow4(Qpa, Qpb, JCH, 5, init_one=True)
                    sc_re = lambda dp: fco.ap()[:, 8 + 2 * dp:9 + 2 * dp]
                    sc_im = lambda dp: fco.ap()[:, 9 + 2 * dp:10 + 2 * dp]
                    for dp in range(4):
                        e.tensor_scalar_mul(LN_(tmpa, dp, 0, 32), Qpa.ap()[:, dp * 32:dp * 32 + 32], sc_re(dp))
                    for dp in range(4):
                        e.tensor_scalar_mul(LN_(tmpb, dp, 0, 32), Qpb.ap()[:, dp * 32:dp * 32 + 32], sc_im(dp))
                    for dp in range(4):
                        e.tensor_sub(LN_(tmpa, dp, 0, 32), LN_(tmpa, dp, 0, 32), LN_(tmpb, dp, 0, 32))
                    for dp in range(4):
                        e.tensor_add(LN_(Sua, dp, 0, 32), SL_u(dp), LN_(tmpa, dp, 0, 32))
                    for dp in range(4):
                        e.tensor_scalar_mul(LN_(tmpc, dp, 0, 32), Qpb.ap()[:, dp * 32:dp * 32 + 32], sc_re(dp))
                    for dp in range(4):
                        e.tensor_scalar_mul(LN_(tmpd, dp, 0, 32), Qpa.ap()[:, dp * 32:dp * 32 + 32], sc_im(dp))
                    for dp in range(4):
                        e.tensor_add(LN_(tmpc, dp, 0, 32), LN_(tmpc, dp, 0, 32), LN_(tmpd, dp, 0, 32))
                    for dp in range(4):
                        e.tensor_add(LN_(Sva, dp, 0, 32), SL_v(dp), LN_(tmpc, dp, 0, 32))
                    last = None
                    for dp in range(4):
                        blk0 = dp * 1024
                        Sbu_ = bufE.ap()[:, 2048:3072].rearrange("p (j k) -> p j k", k=CSZ)
                        Sbv_ = bufE.ap()[:, 3072:4096].rearrange("p (j k) -> p j k", k=CSZ)
                        e.tensor_copy(out=Sbu_, in_=Sua.ap()[:, dp * 32:dp * 32 + 32, None]
                                      .broadcast_to([128, JCH, CSZ]))
                        e.tensor_copy(out=Sbv_, in_=Sva.ap()[:, dp * 32:dp * 32 + 32, None]
                                      .broadcast_to([128, JCH, CSZ]))
                        pu = bufE.ap()[:, 0:1024].rearrange("p (j k) -> p j k", k=CSZ)
                        pv = bufE.ap()[:, 1024:2048].rearrange("p (j k) -> p j k", k=CSZ)
                        Pab = P32a.ap()[:, None, dp * 32:dp * 32 + 32].broadcast_to([128, JCH, CSZ])
                        Pbb = P32b.ap()[:, None, dp * 32:dp * 32 + 32].broadcast_to([128, JCH, CSZ])
                        e.tensor_tensor(out=pu, in0=Sbu_, in1=Pab, op=ALU.mult)
                        e.tensor_tensor(out=pv, in0=Sbv_, in1=Pbb, op=ALU.mult)
                        ub = bufD.ap()[:, blk0:blk0 + 1024]
                        e.tensor_add(ub, ub, bufE.ap()[:, 0:1024])
                        e.tensor_sub(ub, ub, bufE.ap()[:, 1024:2048])
                        last = e.scalar_tensor_tensor(out=bufC[:, blk0:blk0 + 1024],
                                                      in0=ub, scalar=pc(p_scl, l, dp),
                                                      in1=bufC[:, blk0:blk0 + 1024],
                                                      op0=ALU.mult, op1=ALU.add)
                    last.then_inc(s_dve.h, 1)
                    s_dve.n += 1
                    for dp in range(4):
                        M.mark(f"scan_done_{l}_{dp}", s_dve)
                    ln_vector(f"l2_{l}", bufC, y2b(), lambda dp: pc(p_l2w, l, dp),
                              lambda dp: pc(p_l2b, l, dp))
                    for wc in range(4):
                        M.mark(f"w1cast_{l}_{wc}_free", s_dve)
                        e.wait_ge(s_sq.h, M.get(f"w1load_{l}_{wc}"))
                        e.tensor_copy(out=w1b()[:, wc * 2048:(wc + 1) * 2048],
                                      in_=stg[wc % 2]()).then_inc(s_dve.h, 1)
                        s_dve.n += 1
                        M.mark(f"w1cast_{l}_{wc}", s_dve)
                    for wc in range(4):
                        M.mark(f"w2cast_{l}_{wc}_free", s_dve)
                        e.wait_ge(s_sq.h, M.get(f"w2load_{l}_{wc}"))
                        e.tensor_copy(out=w2b()[:, wc * 2048:(wc + 1) * 2048],
                                      in_=stg[wc % 2]()).then_inc(s_dve.h, 1)
                        s_dve.n += 1
                        M.mark(f"w2cast_{l}_{wc}", s_dve)
                    for hh in range(2):
                        for dp in range(4):
                            k = hh * 4 + dp
                            e.wait_ge(s_pe.h, M.get(f"w2mm_{l}_{k}"))
                            blk = slice(dp * 1024 + hh * 512, dp * 1024 + hh * 512 + 512)
                            e.tensor_scalar(out=bufE[:, 0:512], in0=ps[dp][:],
                                            scalar1=pc(p_b2, l, dp), scalar2=None,
                                            op0=ALU.add)
                            e.tensor_add(bufC[:, blk], bufC[:, blk],
                                         bufE[:, 0:512]).then_inc(s_dve.h, 1)
                            s_dve.n += 1
                            M.mark(f"w2ev_{l}_{k}", s_dve)

                ln_vector("lnl", bufC, y2b(), lambda dp: p_lnlw.ap()[:, dp:dp + 1],
                          lambda dp: p_lnlb.ap()[:, dp:dp + 1])
                for wc in range(8):
                    M.mark(f"woutcast_{wc}_free", s_dve)
                    e.wait_ge(s_sq.h, M.get(f"woutload_{wc}"))
                    e.tensor_copy(out=woutb()[:, wc * 2048:(wc + 1) * 2048],
                                  in_=stg[wc % 2]()).then_inc(s_dve.h, 1)
                    s_dve.n += 1
                    M.mark(f"woutcast_{wc}", s_dve)
                e.wait_ge(s_act.h, M.get("attn_sc_done"))
                e.wait_ge(s_aw[0].h, M.get("at_wr_30"))
                e.wait_ge(s_aw[1].h, M.get("at_wr_31"))
                e.memset(tmpa[:, 0:1], 0.0).then_inc(s_dve.h, 1)
                s_dve.n += 1
                M.mark("attn_done", s_dve)

                emit_gating(G_POST)
                M.mark("gating_done", s_dve)

                for hg in range(8):
                    for q in range(4):
                        k = hg * 4 + q
                        e.wait_ge(s_pe.h, M.get(f"dn_acc_{hg}"))
                        e.wait_ge(s_ar[k % 2].h, M.get(f"at_rd_{k}"))
                        ht = hg * 4 + q
                        last = None
                        for hh in range(2):
                            last = e.tensor_add(
                                brsum.ap()[:, ht * 1024 + hh * 512: ht * 1024 + hh * 512 + 512],
                                atsl[k % 2]()[:, hh * 512:hh * 512 + 512],
                                ps[q * 2 + hh][:])
                        last.then_inc(s_dve.h, 1)
                        s_dve.n += 1
                        M.mark(f"dnev_{k}", s_dve)
                M.mark("dn_evac_done", s_dve)

                for tt in range(8):
                    e.wait_ge(s_ox.h, M.get(f"oxload_{tt}"))
                    if tt >= 2:
                        e.wait_ge(s_os[tt % 2].h, M.get(f"ostore_{tt-2}"))
                    dstbuf = bufD if tt % 2 == 0 else bufE
                    for q in range(32):
                        e.wait_ge(s_pe.h, M.get(f"otr_{tt}_{q}"))
                        e.tensor_add(dstbuf[:, q * 128:(q + 1) * 128],
                                     bufC[:, q * 128:(q + 1) * 128],
                                     psb(q % 2)).then_inc(s_dve.h, 1)
                        s_dve.n += 1
                        M.mark(f"otrd_{tt}_{q}", s_dve)
                    M.mark(f"oadd_{tt}", s_dve)

            @block.tensor
            def _(e):
                def emit_g_tiles(pis):
                    for pi_ in pis:
                        k0 = 2 * pi_
                        e.wait_ge(s_gs[k0 % 2].h, M.get(f"gu_slab_{k0}"))
                        e.wait_ge(s_gs[(k0 + 1) % 2].h, M.get(f"gu_slab_{k0 + 1}"))
                        if pi_ >= 1:
                            e.wait_ge(s_act.h, M.get(f"gsilu_{pi_-1}_1"))
                        gmm_last = [None, None]
                        for q in range(32):
                            for hh in range(2):
                                gmm_last[hh] = e.matmul(
                                    ps[4 + hh][:],
                                    gslab[k0 % 2]()[:, q * 128:(q + 1) * 128],
                                    xnT.ap()[:, q * 1024 + hh * 512: q * 1024 + hh * 512 + 512],
                                    start=(q == 0), stop=(q == 31))
                        for hh in range(2):
                            gmm_last[hh].then_inc(s_pe.h, 1)
                            s_pe.n += 1
                            M.mark(f"gmm_{pi_}_{hh}", s_pe)
                        upb = [6, 7] if pi_ % 2 == 0 else [2, 3]
                        if pi_ >= 2:
                            e.wait_ge(s_dve.h, M.get(f"gating_{pi_-2}"))
                        umm_last = [None, None]
                        for q in range(32):
                            for hh in range(2):
                                umm_last[hh] = e.matmul(
                                    ps[upb[hh]][:],
                                    gslab[(k0 + 1) % 2]()[:, q * 128:(q + 1) * 128],
                                    xnT.ap()[:, q * 1024 + hh * 512: q * 1024 + hh * 512 + 512],
                                    start=(q == 0), stop=(q == 31))
                        for hh in range(2):
                            umm_last[hh].then_inc(s_pe.h, 1)
                            s_pe.n += 1
                            M.mark(f"umm_{pi_}_{hh}", s_pe)
                        M.mark(f"gu_mm_{k0}", s_pe)
                        M.mark(f"gu_mm_{k0 + 1}", s_pe)

                e.wait_ge(s_sq.h, M.get("smalls"))
                for tt in range(8):
                    e.wait_ge(s_dve.h, M.get(f"xn_done_{tt}"))
                    for q in range(32):
                        fl = tt * 32 + q
                        if fl >= 2:
                            pq = fl - 2
                            e.wait_ge(s_act.h, M.get(f"xev_{pq // 32}_{pq % 32}"))
                        e.transpose(psb(q % 2), xnb()[:, q * 128:(q + 1) * 128],
                                    idb_sb[:]).then_inc(s_pe.h, 1)
                        s_pe.n += 1
                        M.mark(f"xtr_{tt}_{q}", s_pe)

                e.wait_ge(s_dve.h, M.get("win_ready"))
                e.wait_ge(s_act.h, M.get("xn_all"))
                for dp in range(4):
                    for hh in range(2):
                        k = dp * 2 + hh
                        if k >= 2:
                            e.wait_ge(s_act.h, M.get(f"emb_ev_{k-2}"))
                        for q in range(32):
                            mm = e.matmul(ps[k % 2][:],
                                          winb()[:, q * 512 + dp * 128: q * 512 + (dp + 1) * 128],
                                          xnT.ap()[:, q * 1024 + hh * 512: q * 1024 + hh * 512 + 512],
                                          start=(q == 0), stop=(q == 31))
                        mm.then_inc(s_pe.h, 1)
                        s_pe.n += 1
                        M.mark(f"emb_mm_{k}", s_pe)

                def ln_tensor(tag, src):
                    for dp in range(4):
                        e.wait_ge(s_act.h, M.get(f"{tag}_sq_{dp}"))
                        for hh in range(2):
                            e.matmul(ps[0 + hh][:], ones_sb[:],
                                     src[:, dp * 1024 + hh * 512: dp * 1024 + hh * 512 + 512],
                                     start=(dp == 0), stop=(dp == 3))
                            mm = e.matmul(ps[2 + hh][:], ones_sb[:],
                                          bufE[:, dp * 1024 + hh * 512: dp * 1024 + hh * 512 + 512],
                                          start=(dp == 0), stop=(dp == 3))
                    mm.then_inc(s_pe.h, 1)
                    s_pe.n += 1
                    M.mark(f"{tag}_mm", s_pe)

                for l in range(NL):
                    ln_tensor(f"l1_{l}", bufC)
                    e.wait_ge(s_dve.h, M.get(f"l1_{l}_app_3"))
                    emit_g_tiles(G_SCHED[l])
                    e.wait_ge(s_fg.h, M.get(f"Fgather_{l}"))
                    e.wait_ge(s_dve.h, M.get(f"F_ready_{l}"))
                    for dp in range(4):
                        e.transpose(ps[0].ap()[:, dp * 8:dp * 8 + 8],
                                    agf_sb[0:8, dp * 256:dp * 256 + 128],
                                    idf_sb[:8, :8]).then_inc(s_pe.h, 1)
                        s_pe.n += 1
                        e.transpose(ps[1].ap()[:, dp * 8:dp * 8 + 8],
                                    agf_sb[0:8, dp * 256 + 128:dp * 256 + 256],
                                    idf_sb[:8, :8]).then_inc(s_pe.h, 1)
                        s_pe.n += 1
                    M.mark(f"Ftr_{l}", s_pe)
                    for dp in range(4):
                        e.wait_ge(s_dve.h, M.get(f"scan_done_{l}_{dp}"))
                    ln_tensor(f"l2_{l}", bufC)
                    for hh in range(2):
                        for m in range(16):
                            k = hh * 16 + m
                            e.wait_ge(s_dve.h, M.get(f"l2_{l}_app_3"))
                            e.wait_ge(s_dve.h, M.get(f"w1cast_{l}_3"))
                            if k >= 2:
                                e.wait_ge(s_act.h, M.get(f"w1ev_{l}_{k-2}"))
                            for q in range(4):
                                mm = e.matmul(ps[k % 2][:],
                                              w1b()[:, q * DFF + m * 128: q * DFF + (m + 1) * 128],
                                              y2b()[:, q * 1024 + hh * 512: q * 1024 + hh * 512 + 512],
                                              start=(q == 0), stop=(q == 3))
                            mm.then_inc(s_pe.h, 1)
                            s_pe.n += 1
                            M.mark(f"w1mm_{l}_{k}", s_pe)
                        for dp in range(4):
                            kk2 = hh * 4 + dp
                            e.wait_ge(s_dve.h, M.get(f"w2cast_{l}_3"))
                            e.wait_ge(s_act.h, M.get(f"w1ev_{l}_{hh * 16 + 15}"))
                            if l > 0 or hh > 0:
                                e.wait_ge(s_dve.h, M.get(f"w2ev_{l if hh > 0 else l - 1}_{dp if hh > 0 else 4 + dp}"))
                            for m in range(16):
                                mm = e.matmul(ps[dp][:],
                                              w2b()[:, m * 512 + dp * 128: m * 512 + (dp + 1) * 128],
                                              h1b()[:, m * 512:(m + 1) * 512],
                                              start=(m == 0), stop=(m == 15))
                            mm.then_inc(s_pe.h, 1)
                            s_pe.n += 1
                            M.mark(f"w2mm_{l}_{kk2}", s_pe)

                ln_tensor("lnl", bufC)
                for vt in range(32):
                    for hh in range(2):
                        k = vt * 2 + hh
                        e.wait_ge(s_dve.h, M.get("woutcast_7"))
                        e.wait_ge(s_dve.h, M.get("lnl_app_3"))
                        if k >= 4:
                            e.wait_ge(s_act.h, M.get(f"wo_ev_{(k - 4) // 2}"))
                        for q in range(4):
                            mm = e.matmul(ps[k % 4][:],
                                          woutb()[:, q * 4096 + vt * 128: q * 4096 + (vt + 1) * 128],
                                          y2b()[:, q * 1024 + hh * 512: q * 1024 + hh * 512 + 512],
                                          start=(q == 0), stop=(q == 3))
                        mm.then_inc(s_pe.h, 1)
                        s_pe.n += 1
                        M.mark(f"wo_mm_{k}", s_pe)

                e.wait_ge(s_act.h, M.get("wo_ev_31"))
                emit_g_tiles(G_POST)

                for hg in range(8):
                    if hg >= 1:
                        for q in range(4):
                            e.wait_ge(s_dve.h, M.get(f"dnev_{(hg - 1) * 4 + q}"))
                    for ft in range(NT_G):
                        k = hg * NT_G + ft
                        e.wait_ge(s_ds[k % 4].h, M.get(f"dn_slab_{k}"))
                        for q in range(4):
                            for hh in range(2):
                                mm = e.matmul(ps[q * 2 + hh][:],
                                              wdsl[k % 4]()[:, q * 128:(q + 1) * 128],
                                              hsl[k % 4]()[:, hh * 512:hh * 512 + 512],
                                              start=(ft == 0), stop=(ft == NT_G - 1))
                        mm.then_inc(s_pe.h, 1)
                        s_pe.n += 1
                        M.mark(f"dn_mm_{k}", s_pe)
                    M.mark(f"dn_acc_{hg}", s_pe)

                for tt in range(8):
                    for q in range(32):
                        fl = tt * 32 + q
                        if fl == 0:
                            e.wait_ge(s_dve.h, M.get("dn_evac_done"))
                        if fl >= 2:
                            e.wait_ge(s_dve.h, M.get(f"otrd_{(fl - 2) // 32}_{(fl - 2) % 32}"))
                        e.transpose(psb(q % 2),
                                    brsum.ap()[:, q * 1024 + tt * 128: q * 1024 + tt * 128 + 128],
                                    idb_sb[:]).then_inc(s_pe.h, 1)
                        s_pe.n += 1
                        M.mark(f"otr_{tt}_{q}", s_pe)

            @block.gpsimd
            def _(e):
                for entry in cc_order:
                    if entry[0] == "gu":
                        i = entry[1]
                        e.wait_ge(s_qo.h, M.get(f"dq_out_{2 * i + 1}"))
                        e.collective_compute(
                            "AllGather", ALU.bypass,
                            replica_groups=[list(range(CORES))],
                            ins=[aggu_in[i].ap().opt()],
                            outs=[aggu_out[i].ap().opt()]).then_inc(s_cc.h, 1)
                        s_cc.n += 1
                    elif entry[0] == "scan":
                        l = entry[1]
                        e.wait_ge(s_dve.h, M.get(f"l1_{l}_app_3"))
                        e.memset(bufE[:], 0.0).then_inc(s_pl.h, 1)
                        s_pl.n += 1
                        M.mark(f"vmemset_{l}", s_pl)
                        e.wait_ge(s_dve.h, M.get(f"F_ready_{l}"))
                        for dp in range(4):
                            e.dma_start(out=agf_in[l].ap()[0, dp * 256:dp * 256 + 128][:, None],
                                        in_=fco[:, 2 * dp:2 * dp + 1]).then_inc(s_fw.h, 16)
                            e.dma_start(out=agf_in[l].ap()[0, dp * 256 + 128:dp * 256 + 256][:, None],
                                        in_=fco[:, 2 * dp + 1:2 * dp + 2]).then_inc(s_fw.h, 16)
                            s_fw.n += 32
                        e.wait_ge(s_fw.h, s_fw.n)
                        e.collective_compute(
                            "AllGather", ALU.bypass,
                            replica_groups=[list(range(CORES))],
                            ins=[agf_in[l].ap().opt()],
                            outs=[agf_out[l].ap().opt()]).then_inc(s_cc.h, 1)
                        s_cc.n += 1
                        e.wait_ge(s_cc.h, cc_idx[("scan", l)])
                        e.dma_start(out=agf_sb, in_=agf_out[l].ap()).then_inc(s_fg.h, 16)
                        s_fg.n += 16
                        M.mark(f"Fgather_{l}", s_fg)
                    else:
                        e.wait_ge(s_do.h, M.get("dd_out_10"))
                        e.collective_compute(
                            "AllGather", ALU.bypass,
                            replica_groups=[list(range(CORES))],
                            ins=[agd_in.ap().opt()],
                            outs=[agd_out.ap().opt()]).then_inc(s_cc.h, 1)
                        s_cc.n += 1

    return nc, M


def prep_inputs(inputs):
    f32 = np.float32

    def perm_d(a):
        return np.ascontiguousarray(
            np.asarray(a).reshape(NL, 4, 128).transpose(2, 0, 1).reshape(128, NL * 4)).astype(f32)

    x = np.asarray(inputs["L0_attn_x"]).astype(f32)
    common = {
        "winT": np.ascontiguousarray(np.asarray(inputs["L0_attn_Win"]).T).astype(f32),
        "w1T": np.ascontiguousarray(np.asarray(inputs["L0_attn_W1"]).transpose(0, 2, 1)).astype(f32),
        "w2T": np.ascontiguousarray(np.asarray(inputs["L0_attn_W2"]).transpose(0, 2, 1)).astype(f32),
        "woutT": np.ascontiguousarray(np.asarray(inputs["L0_attn_Wout"]).T).astype(f32),
        "bin_p": np.ascontiguousarray(np.asarray(inputs["L0_attn_bin"]).reshape(4, 128).T).astype(f32),
        "b1_p": np.ascontiguousarray(np.asarray(inputs["L0_attn_b1"]).reshape(NL, 16, 128)
                                     .transpose(2, 0, 1).reshape(128, NL * 16)).astype(f32),
        "b2_p": perm_d(inputs["L0_attn_b2"]),
        "bout_p": np.ascontiguousarray(np.asarray(inputs["L0_attn_bout"]).reshape(32, 128).T).astype(f32),
        "ln1w_p": perm_d(inputs["L0_attn_ln1_w"]),
        "ln1b_p": perm_d(inputs["L0_attn_ln1_b"]),
        "ln2w_p": perm_d(inputs["L0_attn_ln2_w"]),
        "ln2b_p": perm_d(inputs["L0_attn_ln2_b"]),
        "scl_p": perm_d(inputs["L0_attn_sc_lin"]),
        "pr_p": perm_d(inputs["L0_attn_phz_re"]),
        "pi_p": perm_d(inputs["L0_attn_phz_im"]),
        "lcr_p": perm_d(inputs["L0_attn_lc_re"]),
        "lci_p": perm_d(inputs["L0_attn_lc_im"]),
        "lnlw_p": np.ascontiguousarray(np.asarray(inputs["L0_attn_lnl_w"]).reshape(4, 128).T).astype(f32),
        "lnlb_p": np.ascontiguousarray(np.asarray(inputs["L0_attn_lnl_b"]).reshape(4, 128).T).astype(f32),
        "idb": np.eye(128, dtype=ml_dtypes.bfloat16),
        "idf": np.eye(128, dtype=np.float32),
        "ones512": np.full((128, 128), 1.0 / 512, dtype=f32),
        "ceps": np.full((128, 1), 1e-5, dtype=f32),
        "cmh": np.full((128, 1), -0.5, dtype=f32),
    }
    qwgu = np.asarray(inputs["L0_ffn_qweight_gu"])
    scgu = np.asarray(inputs["L0_ffn_scales_gu"])
    qzgu = np.asarray(inputs["L0_ffn_qzeros_gu"])
    qwd = np.asarray(inputs["L0_ffn_qweight_down"])
    scdo = np.asarray(inputs["L0_ffn_scales_down"])
    qzdo = np.asarray(inputs["L0_ffn_qzeros_down"])

    in_maps = []
    for c in range(CORES):
        m = dict(common)
        m["x_tok"] = np.ascontiguousarray(x[0, c * TOK:(c + 1) * TOK, :])
        m["qwgu"] = np.ascontiguousarray(qwgu[c * 64:(c + 1) * 64, :])
        m["scgu"] = np.ascontiguousarray(scgu[c * 4:(c + 1) * 4, :]).astype(f32)
        m["qzgu"] = np.ascontiguousarray(qzgu[c * 4:(c + 1) * 4, :])
        m["qwd"] = np.ascontiguousarray(qwd[:, c * 512:(c + 1) * 512])
        m["scd"] = np.ascontiguousarray(scdo[:, c * 512:(c + 1) * 512]).astype(f32)
        m["qzd"] = np.ascontiguousarray(qzdo[:, c * 64:(c + 1) * 64])
        s = np.zeros((1, 8), f32)
        s[0, c] = 1.0
        m["sel"] = s
        in_maps.append(m)
    return in_maps


def kernel(**inputs):
    in_maps = prep_inputs(inputs)
    _, marks = build(None)
    nc, _ = build(marks)
    res = run_bass_kernel_spmd(nc, in_maps, core_ids=list(range(CORES)))
    out = np.empty((1, CORES * TOK, H), np.float32)
    for c in range(CORES):
        out[0, c * TOK:(c + 1) * TOK] = res.results[c]["out"]
    return out
